# version 05
# baseline (speedup 1.0000x reference)
import numpy as np
import ml_dtypes
from contextlib import ExitStack
import concourse.bass as bass
import concourse.mybir as mybir
from concourse.bass_utils import run_bass_kernel_spmd

F32 = mybir.dt.float32
BF16 = mybir.dt.bfloat16
U8 = mybir.dt.uint8
AX = mybir.AxisListType.X
AF = mybir.ActivationFunctionType
ALU = mybir.AluOpType
BF16NP = ml_dtypes.bfloat16

D = 4096
FF = 11008
HQ = 32
HKV = 8
HD = 128
NRC = D // 128
NCC = FF // 128
TSRC = 2048
TOWN = 1024
NW = 9
SM = float(np.float32(HD ** -0.5))
EPS = 1e-5
NEG = -3.0e38
NY = 15
NG = 12


def _merge(d, tok):
    s, v = tok
    if d.get(s, 0) < v:
        d[s] = v


class Buf:
    __slots__ = ("ap", "wrs", "rds")

    def __init__(self, ap):
        self.ap = ap
        self.wrs = {}
        self.rds = {}


class KB:
    STREAMS = ("y", "g", "t", "v", "s")

    def __init__(self, nc, sems):
        self.nc = nc
        self.sems = sems
        self.cnt = {k: 0 for k in sems}
        self.steps = {k: [] for k in self.STREAMS}
        self.sat = {k: {} for k in self.STREAMS}

    def _emit(self, stream, fn, deps, inc):
        waits = []
        sat = self.sat[stream]
        for s, v in deps.items():
            if sat.get(s, 0) >= v:
                continue
            waits.append((s, v))
            sat[s] = v
        tok = None
        if inc is not None:
            sem, n = inc
            self.cnt[sem] += n
            tok = (sem, self.cnt[sem])
        self.steps[stream].append((fn, waits, inc))
        return tok

    def _op(self, stream, sem, fn, reads, writes, wpart, wdep, deps, inc):
        dl = {}
        for t in deps:
            if t is not None:
                _merge(dl, t)
        for b in reads:
            for kv in b.wrs.items():
                _merge(dl, kv)
        for b in list(writes) + list(wpart) + list(wdep):
            for kv in b.wrs.items():
                _merge(dl, kv)
            for kv in b.rds.items():
                _merge(dl, kv)
        tok = self._emit(stream, fn, dl, (sem, 1) if inc else None)
        if tok is not None:
            for b in reads:
                _merge(b.rds, tok)
            for b in writes:
                b.wrs = dict([tok])
                b.rds = {}
            for b in wpart:
                _merge(b.wrs, tok)
        return tok

    def t(self, fn, reads=(), writes=(), wpart=(), wdep=(), deps=(), inc=True):
        return self._op("t", "t", fn, reads, writes, wpart, wdep, deps, inc)

    def v(self, fn, reads=(), writes=(), wpart=(), wdep=(), deps=(), inc=True):
        return self._op("v", "v", fn, reads, writes, wpart, wdep, deps, inc)

    def s(self, fn, reads=(), writes=(), wpart=(), wdep=(), deps=(), inc=True):
        return self._op("s", "s", fn, reads, writes, wpart, wdep, deps, inc)

    def gc(self, fn, reads=(), writes=(), wpart=(), wdep=(), deps=(), inc=True):
        return self._op("g", "gc", fn, reads, writes, wpart, wdep, deps, inc)

    def _dma(self, stream, ch, dst, src, deps, wpart, dst_ap=None):
        sem = f"{stream}{ch}"
        dl = {}
        for t in deps:
            if t is not None:
                _merge(dl, t)
        reads, writes, wparts = [], [], []
        if isinstance(src, Buf):
            reads.append(src)
            src_ap = src.ap
        else:
            src_ap = src
        if isinstance(dst, Buf):
            (wparts if wpart else writes).append(dst)
            if dst_ap is None:
                dst_ap = dst.ap
        else:
            dst_ap = dst
        for b in reads:
            for kv in b.wrs.items():
                _merge(dl, kv)
        for b in writes + wparts:
            for kv in b.wrs.items():
                _merge(dl, kv)
            for kv in b.rds.items():
                _merge(dl, kv)
        fn = lambda e, o=dst_ap, i=src_ap: e.dma_start(out=o, in_=i)
        tok = self._emit(stream, fn, dl, (sem, 16))
        for b in reads:
            _merge(b.rds, tok)
        for b in writes:
            b.wrs = dict([tok])
            b.rds = {}
        for b in wparts:
            _merge(b.wrs, tok)
        return tok

    def y(self, dst, src, ch, deps=(), wpart=False, dst_ap=None):
        return self._dma("y", ch, dst, src, deps, wpart, dst_ap)

    def g(self, dst, src, ch, deps=(), wpart=False, dst_ap=None):
        return self._dma("g", ch, dst, src, deps, wpart, dst_ap)

    def barrier(self):
        for st, sem in (("t", "t"), ("v", "v"), ("s", "s"), ("g", "gc")):
            self._op(st, sem, lambda e: e.nop(), (), (), (), (),
                     [(sem, self.cnt[sem])], True)
        tgt = {k: v for k, v in self.cnt.items() if v > 0}
        for st in self.STREAMS:
            self._emit(st, None, dict(tgt), None)
            self.sat[st] = dict(tgt)

    def flush(self):
        nc = self.nc
        sems = self.sems
        steps = self.steps
        self.steps = {k: [] for k in self.STREAMS}

        def mk(stream):
            def body(eng):
                for fn, waits, inc in steps[stream]:
                    for (s, v) in waits:
                        eng.wait_ge(sems[s], v)
                    if fn is None:
                        continue
                    ins = fn(eng)
                    if inc is not None:
                        ins.then_inc(sems[inc[0]], inc[1])
            return body

        with nc.Block() as blk:
            blk.sync(mk("y"))
            blk.gpsimd(mk("g"))
            blk.tensor(mk("t"))
            blk.vector(mk("v"))
            blk.scalar(mk("s"))


def bufs_of(sb, n, width):
    return [Buf(sb[:, i * width:(i + 1) * width]) for i in range(n)]


def _host_masks(h):
    m = np.zeros((8, 128, NW * 128), np.float32)
    ii = np.arange(128)[:, None]
    jj = np.arange(128)[None, :]
    UT = np.where(jj > ii, np.float32(0), np.float32(NEG))
    LT = np.where(jj <= ii, np.float32(0), np.float32(NEG))
    for l in range(8):
        for w in range(NW):
            blk = m[l][:, w * 128:(w + 1) * 128]
            if w == 8:
                blk[:] = LT
            elif h == 1:
                blk[:] = UT if w == 0 else 0.0
            else:
                blk[:] = NEG if w < 8 - l else 0.0
    return np.ascontiguousarray(m.reshape(8 * 128, NW * 128))


def _host_rope(h):
    inv = np.power(10000.0, -np.arange(64) / 64.0)
    pos = (h * 1024 - 1024) + np.arange(2048, dtype=np.float64)
    ang = pos[:, None] * inv[None, :]
    c = np.cos(ang).T
    s = np.sin(ang).T
    C = np.vstack([c, c])
    S = np.vstack([-s, s])
    Ch = np.ascontiguousarray(C.astype(np.float32))
    Cl = np.ascontiguousarray((C - Ch).astype(BF16NP))
    Sh = np.ascontiguousarray(S.astype(np.float32))
    Sl = np.ascontiguousarray((S - Sh).astype(BF16NP))
    return Ch, Cl, Sh, Sl


def _host_gsel(n):
    g = np.zeros((128, n * n), BF16NP)
    for i in range(n):
        g[:, i * n + i] = 1
    return g


def build_program(upto=5, dbg=False):
    nc = bass.Bass("TRN2", target_bir_lowering=False)

    def DI(name, shape, dt):
        return nc.dram_tensor(name, shape, dt, kind="ExternalInput").ap()

    xkv = DI("xkv", [TSRC, D], F32)
    wq = DI("wq", [D, D], F32)
    wk = DI("wk", [D, HKV * HD], F32)
    wv = DI("wv", [D, HKV * HD], F32)
    wo = DI("wo", [D, D], F32)
    qwg = DI("qwg", [D, FF], U8)
    qwu = DI("qwu", [D, FF], U8)
    qwd = DI("qwd", [FF, D], U8)
    zg = DI("zg", [NRC, FF], U8)
    zu = DI("zu", [NRC, FF], U8)
    zd = DI("zd", [NCC, D], U8)
    sg = DI("sg", [NRC, FF], F32)
    su = DI("su", [NRC, FF], F32)
    sd = DI("sd", [NCC, D], F32)
    masks = DI("masks", [8 * 128, NW * 128], F32)
    ropeCh = DI("ropeCh", [128, TSRC], F32)
    ropeCl = DI("ropeCl", [128, TSRC], BF16)
    ropeSh = DI("ropeSh", [128, TSRC], F32)
    ropeSl = DI("ropeSl", [128, TSRC], BF16)
    gsel32 = DI("gsel32", [128, 32 * 32], BF16)
    gsel86 = DI("gsel86", [128, 86 * 86], BF16)
    identf = DI("identf", [128, 128], F32)
    identb = DI("identb", [128, 128], BF16)

    out = nc.dram_tensor("out", [TOWN, D], F32, kind="ExternalOutput").ap()

    skind = dict(kind="ExternalOutput") if dbg else {}

    def SCR(name, shape, dt):
        return nc.dram_tensor(name, shape, dt, **skind).ap()

    kscr_h = SCR("kscr_h", [HKV * 128, TSRC], BF16)
    kscr_l = SCR("kscr_l", [HKV * 128, TSRC], BF16)
    vscr_h = SCR("vscr_h", [16 * 128, HKV * HD], BF16)
    vscr_l = SCR("vscr_l", [16 * 128, HKV * HD], BF16)
    xnscr_h = SCR("xnscr_h", [NRC * 128, TOWN], BF16)
    xnscr_l = SCR("xnscr_l", [NRC * 128, TOWN], BF16)
    qscr_h = SCR("qscr_h", [HQ * 128, TOWN], BF16)
    qscr_l = SCR("qscr_l", [HQ * 128, TOWN], BF16)
    oscr_h = SCR("oscr_h", [HQ * 128, TOWN], BF16)
    oscr_l = SCR("oscr_l", [HQ * 128, TOWN], BF16)
    sscr = SCR("sscr", [NRC * 128, TOWN], F32)
    hscr_h = SCR("hscr_h", [NCC * 128, TOWN], BF16)
    hscr_l = SCR("hscr_l", [NCC * 128, TOWN], BF16)
    nzsg_h = SCR("nzsg_h", [NRC, FF], BF16)
    nzsg_l = SCR("nzsg_l", [NRC, FF], BF16)
    nzsu_h = SCR("nzsu_h", [NRC, FF], BF16)
    nzsu_l = SCR("nzsu_l", [NRC, FF], BF16)
    nzsd_h = SCR("nzsd_h", [NCC, D], BF16)
    nzsd_l = SCR("nzsd_l", [NCC, D], BF16)

    with ExitStack() as es:
        sems = {}
        for name in (["t", "v", "s", "gc"] + [f"y{i}" for i in range(NY)]
                     + [f"g{i}" for i in range(NG)]):
            sems[name] = es.enter_context(nc.semaphore("sem_" + name))
        kb = KB(nc, sems)

        idf = Buf(es.enter_context(nc.sbuf_tensor("idf", [128, 128], F32))[:])
        idb = Buf(es.enter_context(nc.sbuf_tensor("idb", [128, 128], BF16))[:])
        epsb = Buf(es.enter_context(nc.sbuf_tensor("epsb", [128, 1], F32))[:])
        kb.y(idf, identf[:, :], 14)
        kb.y(idb, identb[:, :], 14)
        idf.wrs = {"y14": 32}
        idb.wrs = {"y14": 32}
        kb.gc(lambda e, o=epsb.ap: e.memset(o, EPS), writes=[epsb])

        def split_w(wb, wh, wl):
            kb.s(lambda e, o=wh.ap, i=wb.ap: e.copy(o, i), reads=[wb], writes=[wh])
            kb.v(lambda e, o=wl.ap, i=wb.ap, hh=wh.ap:
                 e.scalar_tensor_tensor(o, i, 1.0, hh, ALU.mult, ALU.subtract),
                 reads=[wb, wh], writes=[wl])

        def rope(psrc, cs, rCh, rCl, rSh, rSl, kf, rot, t1, t2, t3, oh_, ol_):
            kb.v(lambda e, o=kf.ap, i=psrc.ap: e.tensor_copy(o, i),
                 reads=[psrc], writes=[kf])
            kb.v(lambda e, o=rot.ap[0:64, :], i=kf.ap[64:128, :]: e.tensor_copy(o, i),
                 reads=[kf], wpart=[rot])
            kb.v(lambda e, o=rot.ap[64:128, :], i=kf.ap[0:64, :]: e.tensor_copy(o, i),
                 reads=[kf], wpart=[rot])
            kb.v(lambda e, o=t1.ap, a=kf.ap, b=rCh.ap[:, cs]:
                 e.tensor_tensor(o, a, b, ALU.mult), reads=[kf, rCh], writes=[t1])
            kb.v(lambda e, o=t2.ap, a=rot.ap, b=rSh.ap[:, cs]:
                 e.tensor_tensor(o, a, b, ALU.mult), reads=[rot, rSh], writes=[t2])
            kb.v(lambda e, o=t3.ap, a=t1.ap, b=t2.ap:
                 e.tensor_tensor(o, a, b, ALU.add), reads=[t1, t2], writes=[t3])
            kb.v(lambda e, o=t1.ap, a=kf.ap, b=rCl.ap[:, cs]:
                 e.tensor_tensor(o, a, b, ALU.mult), reads=[kf, rCl], writes=[t1])
            kb.v(lambda e, o=t2.ap, a=rot.ap, b=rSl.ap[:, cs]:
                 e.tensor_tensor(o, a, b, ALU.mult), reads=[rot, rSl], writes=[t2])
            kb.v(lambda e, o=t1.ap, a=t1.ap, b=t2.ap:
                 e.tensor_tensor(o, a, b, ALU.add), reads=[t1, t2], writes=[t1])
            kb.v(lambda e, o=kf.ap, a=t3.ap, b=t1.ap:
                 e.tensor_tensor(o, a, b, ALU.add), reads=[t3, t1], writes=[kf])
            split_w(kf, oh_, ol_)

        with ExitStack() as ps:
            sb = lambda n, shp, dt: ps.enter_context(nc.sbuf_tensor(n, shp, dt))[:]
            rCh = Buf(sb("rCh", [128, TSRC], F32))
            rCl = Buf(sb("rCl", [128, TSRC], BF16))
            rSh = Buf(sb("rSh", [128, TSRC], F32))
            rSl = Buf(sb("rSl", [128, TSRC], BF16))
            kb.y(rCh, ropeCh[:, :], 4)
            kb.y(rCl, ropeCl[:, :], 4)
            kb.y(rSh, ropeSh[:, :], 4)
            kb.y(rSl, ropeSl[:, :], 4)
            ld_all = ("y4", kb.cnt["y4"])
            for b in (rCh, rCl, rSh, rSl):
                b.wrs = dict([ld_all])
            xt_p = [Buf(sb(f"xt{i}", [128, D], F32)) for i in range(2)]
            sq_p = [Buf(sb("sq", [128, D], F32))] * 2
            st1_p = [Buf(sb(f"st1{i}", [128, 1], F32)) for i in range(2)]
            stss_p = [Buf(sb(f"stss{i}", [128, 1], F32)) for i in range(2)]
            stm_p = [Buf(sb(f"stm{i}", [128, 1], F32)) for i in range(2)]
            stm2_p = [Buf(sb(f"stm2{i}", [128, 1], F32)) for i in range(2)]
            stvar_p = [Buf(sb(f"stvar{i}", [128, 1], F32)) for i in range(2)]
            stsd_p = [Buf(sb(f"stsd{i}", [128, 1], F32)) for i in range(2)]
            strr_p = [Buf(sb(f"str_{i}", [128, 1], F32)) for i in range(2)]
            stnb_p = [Buf(sb(f"stnb{i}", [128, 1], F32)) for i in range(2)]
            xnf_p = [Buf(sb(f"xnf{i}", [128, D], F32)) for i in range(2)]
            xnT_h = bufs_of(sb("xnT_h", [128, NRC * 512], BF16), NRC, 512)
            xnT_l = bufs_of(sb("xnT_l", [128, NRC * 512], BF16), NRC, 512)
            wbuf_p = [Buf(sb(f"wb{i}", [128, 1024], F32)) for i in range(2)]
            wh_p = [Buf(sb(f"whp{i}", [128, 1024], BF16)) for i in range(2)]
            wl_p = [Buf(sb(f"wlp{i}", [128, 1024], BF16)) for i in range(2)]
            kf = Buf(sb("kf", [128, 512], F32))
            rot = Buf(sb("rot", [128, 512], F32))
            t1 = Buf(sb("t1_", [128, 512], F32))
            t2 = Buf(sb("t2_", [128, 512], F32))
            t3 = Buf(sb("t3_", [128, 512], F32))
            kh = Buf(sb("kh", [128, 512], BF16))
            kl = Buf(sb("kl", [128, 512], BF16))
            vh = Buf(sb("vh", [128, 512], BF16))
            vl = Buf(sb("vl", [128, 512], BF16))
            pb = [Buf(ps.enter_context(nc.psum_tensor(f"pb{i}", [128, 512], F32))[:])
                  for i in range(8)]

            for tc in range(4):
                for tt in range(4):
                    xt = xt_p[tt % 2]
                    sq, st1, stss = sq_p[tt % 2], st1_p[tt % 2], stss_p[tt % 2]
                    stm, stm2, stvar = stm_p[tt % 2], stm2_p[tt % 2], stvar_p[tt % 2]
                    stsd, str_, stnb = stsd_p[tt % 2], strr_p[tt % 2], stnb_p[tt % 2]
                    xnf = xnf_p[tt % 2]
                    kb.y(xt, xkv[tc * 512 + tt * 128: tc * 512 + (tt + 1) * 128, :], tt % 2)
                    kb.v(lambda e, a=xt.ap, o=st1.ap: e.reduce_sum(o, a, axis=AX),
                         reads=[xt], writes=[st1])
                    kb.s(lambda e, a=xt.ap, o=sq.ap, ac=stss.ap:
                         e.activation(o, a, AF.Square, accum_out=ac),
                         reads=[xt], writes=[sq, stss])
                    kb.v(lambda e, a=st1.ap, o=stm.ap:
                         e.tensor_scalar(o, a, 1.0 / D, None, ALU.mult),
                         reads=[st1], writes=[stm])
                    kb.v(lambda e, a=stm.ap, o=stm2.ap: e.tensor_tensor(o, a, a, ALU.mult),
                         reads=[stm], writes=[stm2])
                    kb.v(lambda e, a=stss.ap, m2=stm2.ap, o=stvar.ap:
                         e.scalar_tensor_tensor(o, a, 1.0 / D, m2, ALU.mult, ALU.subtract),
                         reads=[stss, stm2], writes=[stvar])
                    kb.s(lambda e, a=stvar.ap, o=stsd.ap, eb=epsb.ap:
                         e.activation(o, a, AF.Sqrt, bias=eb),
                         reads=[stvar, epsb], writes=[stsd])
                    kb.v(lambda e, a=stsd.ap, o=str_.ap: e.reciprocal(o, a),
                         reads=[stsd], writes=[str_])
                    kb.v(lambda e, m=stm.ap, r=str_.ap, o=stnb.ap:
                         e.tensor_scalar(o, m, r, -1.0, ALU.mult, ALU.mult),
                         reads=[stm, str_], writes=[stnb])
                    kb.v(lambda e, a=xt.ap, r=str_.ap, b=stnb.ap, o=xnf.ap:
                         e.tensor_scalar(o, a, r, b, ALU.mult, ALU.add),
                         reads=[xt, str_, stnb], writes=[xnf])
                    for rc in range(NRC):
                        p = pb[rc % 2]
                        kb.t(lambda e, o=p.ap[:, 0:128],
                             i=xnf.ap[:, rc * 128:(rc + 1) * 128], ident=idf.ap:
                             e.transpose(o, i, ident),
                             reads=[xnf, idf], writes=[p])
                        kb.s(lambda e, o=xnT_h[rc].ap[:, tt * 128:(tt + 1) * 128],
                             i=p.ap[:, 0:128]: e.copy(o, i),
                             reads=[p], wpart=[xnT_h[rc]])
                        kb.v(lambda e, o=xnT_l[rc].ap[:, tt * 128:(tt + 1) * 128],
                             i=p.ap[:, 0:128], hh=xnT_h[rc].ap[:, tt * 128:(tt + 1) * 128]:
                             e.scalar_tensor_tensor(o, i, 1.0, hh, ALU.mult, ALU.subtract),
                             reads=[p, xnT_h[rc]], wpart=[xnT_l[rc]])

                for rc in range(NRC):
                    wb, wh, wl = wbuf_p[rc % 2], wh_p[rc % 2], wl_p[rc % 2]
                    kb.y(wb, wk[rc * 128:(rc + 1) * 128, :], 2 + rc % 2)
                    split_w(wb, wh, wl)
                    first = rc == 0
                    last = rc == NRC - 1
                    for kvh in range(HKV):
                        p = pb[kvh]
                        for j, xs in enumerate((xnT_h[rc], xnT_l[rc])):
                            kb.t(lambda e, o=p.ap, w=wh.ap[:, kvh * 128:(kvh + 1) * 128],
                                 r=xs.ap, s0=(first and j == 0):
                                 e.matmul(o, w, r, start=s0, stop=False),
                                 reads=(wh, xnT_h[rc], xnT_l[rc]),
                                 wdep=[p] if (first and j == 0) else (),
                                 inc=False)
                    for kvh in range(HKV):
                        p = pb[kvh]
                        kb.t(lambda e, o=p.ap, w=wl.ap[:, kvh * 128:(kvh + 1) * 128],
                             r=xnT_h[rc].ap, s1=last:
                             e.matmul(o, w, r, start=False, stop=s1),
                             reads=(wh, wl, xnT_h[rc], xnT_l[rc]),
                             writes=[p] if last else (),
                             inc=True)
                for kvh in range(HKV):
                    cs = slice(tc * 512, (tc + 1) * 512)
                    rope(pb[kvh], cs, rCh, rCl, rSh, rSl, kf, rot, t1, t2, t3, kh, kl)
                    kb.g(kscr_h[kvh * 128:(kvh + 1) * 128, cs], kh, 0)
                    kb.g(kscr_l[kvh * 128:(kvh + 1) * 128, cs], kl, 1)

                for rc in range(NRC):
                    wb, wh, wl = wbuf_p[rc % 2], wh_p[rc % 2], wl_p[rc % 2]
                    kb.y(wb, wv[rc * 128:(rc + 1) * 128, :], 2 + rc % 2)
                    split_w(wb, wh, wl)
                    first = rc == 0
                    last = rc == NRC - 1
                    for kt in range(4):
                        for wt, dvc in ((wh, 0), (wh, 1), (wl, 0), (wl, 1)):
                            p = pb[kt * 2 + dvc]
                            st0 = first and wt is wh
                            kb.t(lambda e, o=p.ap, w=xnT_h[rc].ap[:, kt * 128:(kt + 1) * 128],
                                 r=wt.ap[:, dvc * 512:(dvc + 1) * 512], s0=st0:
                                 e.matmul(o, w, r, start=s0, stop=False),
                                 reads=(wh, wl, xnT_h[rc]),
                                 wdep=[p] if st0 else (),
                                 inc=False)
                    for kt in range(4):
                        for dvc in range(2):
                            p = pb[kt * 2 + dvc]
                            kb.t(lambda e, o=p.ap, w=xnT_l[rc].ap[:, kt * 128:(kt + 1) * 128],
                                 r=wh.ap[:, dvc * 512:(dvc + 1) * 512], s1=last:
                                 e.matmul(o, w, r, start=False, stop=s1),
                                 reads=(wh, wl, xnT_h[rc], xnT_l[rc]),
                                 writes=[p] if last else (),
                                 inc=True)
                for kt in range(4):
                    slot = tc * 4 + kt
                    for dvc in range(2):
                        p = pb[kt * 2 + dvc]
                        split_w(p, vh, vl)
                        ds = slice(dvc * 512, (dvc + 1) * 512)
                        kb.g(vscr_h[slot * 128:(slot + 1) * 128, ds], vh, 2)
                        kb.g(vscr_l[slot * 128:(slot + 1) * 128, ds], vl, 3)

                if tc >= 2:
                    ts_ = slice((tc - 2) * 512, (tc - 1) * 512)
                    for rc in range(NRC):
                        kb.g(xnscr_h[rc * 128:(rc + 1) * 128, ts_], xnT_h[rc], 4)
                        kb.g(xnscr_l[rc * 128:(rc + 1) * 128, ts_], xnT_l[rc], 5)
                    for rc in range(NRC):
                        _merge(xnT_h[rc].rds, ("g4", kb.cnt["g4"]))
                        _merge(xnT_l[rc].rds, ("g5", kb.cnt["g5"]))
            kb.barrier()
            kb.flush()
        if upto <= 1:
            return nc

        with ExitStack() as ps:
            sb = lambda n, shp, dt: ps.enter_context(nc.sbuf_tensor(n, shp, dt))[:]
            rCh = Buf(sb("rCh2", [128, TOWN], F32))
            rCl = Buf(sb("rCl2", [128, TOWN], BF16))
            rSh = Buf(sb("rSh2", [128, TOWN], F32))
            rSl = Buf(sb("rSl2", [128, TOWN], BF16))
            kb.y(rCh, ropeCh[:, 1024:2048], 4)
            kb.y(rCl, ropeCl[:, 1024:2048], 4)
            kb.y(rSh, ropeSh[:, 1024:2048], 4)
            kb.y(rSl, ropeSl[:, 1024:2048], 4)
            xnT_h = bufs_of(sb("xnT_h2", [128, NRC * 1024], BF16), NRC, 1024)
            xnT_l = bufs_of(sb("xnT_l2", [128, NRC * 1024], BF16), NRC, 1024)
            for rc in range(NRC):
                kb.y(xnT_h[rc], xnscr_h[rc * 128:(rc + 1) * 128, :], 4)
                kb.y(xnT_l[rc], xnscr_l[rc * 128:(rc + 1) * 128, :], 4)
            ld_all = ("y4", kb.cnt["y4"])
            for b in [rCh, rCl, rSh, rSl] + xnT_h + xnT_l:
                b.wrs = dict([ld_all])
            wbuf_p = [Buf(sb(f"wb2{i}", [128, 512], F32)) for i in range(2)]
            wh_p = [Buf(sb(f"wh2{i}", [128, 512], BF16)) for i in range(2)]
            wl_p = [Buf(sb(f"wl2{i}", [128, 512], BF16)) for i in range(2)]
            kf = Buf(sb("kf2", [128, 512], F32))
            rot = Buf(sb("rot2", [128, 512], F32))
            t1 = Buf(sb("t1_2", [128, 512], F32))
            t2 = Buf(sb("t2_2", [128, 512], F32))
            t3 = Buf(sb("t3_2", [128, 512], F32))
            kh = Buf(sb("kh2", [128, 512], BF16))
            kl = Buf(sb("kl2", [128, 512], BF16))
            pb = [Buf(ps.enter_context(nc.psum_tensor(f"pb2{i}", [128, 512], F32))[:])
                  for i in range(8)]

            for qhg in range(8):
                for rc in range(NRC):
                    wb, wh, wl = wbuf_p[rc % 2], wh_p[rc % 2], wl_p[rc % 2]
                    kb.y(wb, wq[rc * 128:(rc + 1) * 128, qhg * 512:(qhg + 1) * 512], 2 + rc % 2)
                    split_w(wb, wh, wl)
                    first = rc == 0
                    last = rc == NRC - 1
                    for hi in range(4):
                        for xs, tcq in ((xnT_h[rc], 0), (xnT_h[rc], 1),
                                        (xnT_l[rc], 0), (xnT_l[rc], 1)):
                            p = pb[hi * 2 + tcq]
                            st0 = first and xs is xnT_h[rc]
                            kb.t(lambda e, o=p.ap, w=wh.ap[:, hi * 128:(hi + 1) * 128],
                                 r=xs.ap[:, tcq * 512:(tcq + 1) * 512], s0=st0:
                                 e.matmul(o, w, r, start=s0, stop=False),
                                 reads=(wh, xnT_h[rc], xnT_l[rc]),
                                 wdep=[p] if st0 else (),
                                 inc=False)
                    for hi in range(4):
                        for tcq in range(2):
                            p = pb[hi * 2 + tcq]
                            kb.t(lambda e, o=p.ap, w=wl.ap[:, hi * 128:(hi + 1) * 128],
                                 r=xnT_h[rc].ap[:, tcq * 512:(tcq + 1) * 512], s1=last:
                                 e.matmul(o, w, r, start=False, stop=s1),
                                 reads=(wh, wl, xnT_h[rc], xnT_l[rc]),
                                 writes=[p] if last else (),
                                 inc=True)
                for hi in range(4):
                    qh = qhg * 4 + hi
                    for tcq in range(2):
                        p = pb[hi * 2 + tcq]
                        cs = slice(tcq * 512, (tcq + 1) * 512)
                        rope(p, cs, rCh, rCl, rSh, rSl, kf, rot, t1, t2, t3, kh, kl)
                        kb.g(qscr_h[qh * 128:(qh + 1) * 128, cs], kh, 0)
                        kb.g(qscr_l[qh * 128:(qh + 1) * 128, cs], kl, 1)
            kb.barrier()
            kb.flush()
        if upto <= 2:
            return nc

        with ExitStack() as ps:
            sb = lambda n, shp, dt: ps.enter_context(nc.sbuf_tensor(n, shp, dt))[:]
            mks = Buf(sb("mks", [128, 8 * NW * 128], F32))
            for l in range(8):
                kb.y(mks, masks[l * 128:(l + 1) * 128, :], 4, wpart=True,
                     dst_ap=mks.ap[:, l * 1152:(l + 1) * 1152])
            qb_h = [Buf(sb(f"qbh{i}", [128, 512], BF16)) for i in range(2)]
            qb_l = [Buf(sb(f"qbl{i}", [128, 512], BF16)) for i in range(2)]
            kB_h = [Buf(sb(f"kbh{i}", [128, 1536], BF16)) for i in range(2)]
            kB_l = [Buf(sb(f"kbl{i}", [128, 1536], BF16)) for i in range(2)]
            vB_h = [Buf(sb(f"vbh{i}", [128, 1536], BF16)) for i in range(2)]
            vB_l = [Buf(sb(f"vbl{i}", [128, 1536], BF16)) for i in range(2)]
            AT_hp = [Buf(sb(f"ATh{i}", [128, 12 * 512], BF16)) for i in range(2)]
            AT_lp = [Buf(sb(f"ATl{i}", [128, 12 * 512], BF16)) for i in range(2)]
            Af_p = [Buf(sb(f"Af{i}", [128, 1152], F32)) for i in range(2)]
            Ahh_p = [Buf(sb(f"Ahh{i}", [128, 1152], BF16)) for i in range(2)]
            All_p = [Buf(sb(f"All{i}", [128, 1152], BF16)) for i in range(2)]
            Sm_p = [Buf(sb(f"Sm{i}", [128, 1152], F32)) for i in range(2)]
            stm_p = [Buf(sb(f"stm3{i}", [128, 1], F32)) for i in range(2)]
            stnb_p = [Buf(sb(f"stnb3{i}", [128, 1], F32)) for i in range(2)]
            stl_p = [Buf(sb(f"stl3{i}", [128, 1], F32)) for i in range(2)]
            str_p = [Buf(sb(f"str3{i}", [128, 1], F32)) for i in range(2)]
            oh_p = [Buf(sb(f"oh3{i}", [128, 512], BF16)) for i in range(2)]
            ol_p = [Buf(sb(f"ol3{i}", [128, 512], BF16)) for i in range(2)]
            pscT = ps.enter_context(nc.psum_tensor("pscT", [128, 1536], F32))[:]
            psc = [Buf(pscT[:, i * 512:(i + 1) * 512]) for i in range(3)]
            ptr = [Buf(ps.enter_context(nc.psum_tensor(f"ptr{i}", [128, 1024], BF16))[:])
                   for i in range(2)]
            pav = [Buf(ps.enter_context(nc.psum_tensor(f"pav{i}", [128, 512], F32))[:])
                   for i in range(2)]

            for par in range(2):
                for s_ in range(12):
                    for l4 in range(4):
                        if l4 <= s_ <= l4 + 8:
                            continue
                        for ab in (AT_hp[par], AT_lp[par]):
                            kb.gc(lambda e,
                                  o=ab.ap[:, s_ * 512 + l4 * 128: s_ * 512 + (l4 + 1) * 128]:
                                  e.memset(o, 0.0), wpart=[ab])

            for g2 in range(2):
                for kvh in range(HKV):
                    kbh, kbl = kB_h[kvh % 2], kB_l[kvh % 2]
                    vbh, vbl = vB_h[kvh % 2], vB_l[kvh % 2]
                    ks_ = slice(g2 * 512, g2 * 512 + 1536)
                    kb.y(kbh, kscr_h[kvh * 128:(kvh + 1) * 128, ks_], 0 + kvh % 2)
                    kb.y(kbl, kscr_l[kvh * 128:(kvh + 1) * 128, ks_], 2 + kvh % 2)
                    voff = (g2 * 4) * 128 * 1024 + kvh * 128
                    vap = [[1024, 128], [128 * 1024, 12], [1, 128]]
                    kb.y(vbh, bass.AP(vscr_h.tensor, voff, vap), 5 + kvh % 2)
                    kb.y(vbl, bass.AP(vscr_l.tensor, voff, vap), 7 + kvh % 2)
                    for hi in range(4):
                        qh = kvh * 4 + hi
                        qbh, qbl = qb_h[hi % 2], qb_l[hi % 2]
                        qs_ = slice(g2 * 512, (g2 + 1) * 512)
                        kb.y(qbh, qscr_h[qh * 128:(qh + 1) * 128, qs_], 9 + hi % 2)
                        kb.y(qbl, qscr_l[qh * 128:(qh + 1) * 128, qs_], 11 + hi % 2)
                        pav_b = pav[qh % 2]
                        ATh = AT_hp[qh % 2]
                        ATl = AT_lp[qh % 2]

                        def scores_softmax(l4, qbh=qbh, qbl=qbl, kbh=kbh, kbl=kbl, g2=g2):
                            l = g2 * 4 + l4
                            stm = stm_p[l4 % 2]
                            stnb = stnb_p[l4 % 2]
                            stl = stl_p[l4 % 2]
                            Af = Af_p[l4 % 2]
                            Sm = Sm_p[l4 % 2]
                            chunks = [(0, 512), (512, 512), (1024, 128)]
                            for qq, kk2, st0, sp1 in ((qbh, kbh, True, False),
                                                      (qbh, kbl, False, False),
                                                      (qbl, kbh, False, True)):
                                for ci, (c0, cw) in enumerate(chunks):
                                    p = psc[ci]
                                    kb.t(lambda e, o=p.ap[:, 0:cw],
                                         w=qq.ap[:, l4 * 128:(l4 + 1) * 128],
                                         r=kk2.ap[:, l4 * 128 + c0: l4 * 128 + c0 + cw],
                                         s0=st0, s1=sp1:
                                         e.matmul(o, w, r, start=s0, stop=s1),
                                         reads=(qbh, qbl, kbh, kbl),
                                         writes=[p] if sp1 else (),
                                         wdep=[p] if st0 else (),
                                         inc=sp1)
                            kb.v(lambda e, o=Sm.ap, a=pscT[:, 0:1152],
                                 m=mks.ap[:, l * 1152:(l + 1) * 1152]:
                                 e.tensor_tensor(o, a, m, ALU.add),
                                 reads=[mks, psc[0], psc[1], psc[2]],
                                 writes=[Sm])
                            kb.v(lambda e, o=stm.ap, i=Sm.ap: e.reduce_max(o, i, axis=AX),
                                 reads=[Sm], writes=[stm])
                            kb.v(lambda e, o=stnb.ap, i=stm.ap:
                                 e.tensor_scalar(o, i, -SM, None, ALU.mult),
                                 reads=[stm], writes=[stnb])
                            kb.s(lambda e, o=Af.ap, i=Sm.ap, b=stnb.ap, ls=stl.ap:
                                 e.activation(o, i, AF.Exp, bias=b, scale=SM,
                                              accum_out=ls),
                                 reads=[Sm, stnb],
                                 writes=[Af, stl])

                        def split_transpose(l4, ATh=ATh, ATl=ATl):
                            stl = stl_p[l4 % 2]
                            str_ = str_p[l4 % 2]
                            Af = Af_p[l4 % 2]
                            Ahh = Ahh_p[l4 % 2]
                            All = All_p[l4 % 2]
                            kb.v(lambda e, o=str_.ap, i=stl.ap: e.reciprocal(o, i),
                                 reads=[stl], writes=[str_])
                            kb.v(lambda e, o=Ahh.ap, i=Af.ap, r=str_.ap:
                                 e.tensor_scalar(o, i, r, None, ALU.mult),
                                 reads=[Af, str_], writes=[Ahh])
                            kb.v(lambda e, o=All.ap, i=Af.ap, r=str_.ap, hh=Ahh.ap:
                                 e.scalar_tensor_tensor(o, i, r, hh, ALU.mult, ALU.subtract),
                                 reads=[Af, str_, Ahh], writes=[All])
                            for w in range(NW):
                                s_ = l4 + w
                                for j, (src, dst) in enumerate(((Ahh, ATh), (All, ATl))):
                                    p = ptr[(2 * w + j) % 2]
                                    kb.t(lambda e, o=p.ap[:, 0:128],
                                         i=src.ap[:, w * 128:(w + 1) * 128], ident=idb.ap:
                                         e.transpose(o, i, ident),
                                         reads=[src, idb], writes=[p])
                                    dsl_ap = dst.ap[:, s_ * 512 + l4 * 128:
                                                    s_ * 512 + (l4 + 1) * 128]
                                    if j == 0:
                                        kb.s(lambda e, o=dsl_ap, i=p.ap[:, 0:128]:
                                             e.copy(o, i), reads=[p], wpart=[dst])
                                    else:
                                        kb.v(lambda e, o=dsl_ap, i=p.ap[:, 0:128]:
                                             e.tensor_copy(o, i), reads=[p], wpart=[dst])

                        for step in range(5):
                            if step >= 1:
                                split_transpose(step - 1)
                            if step < 4:
                                scores_softmax(step)
                        for s_ in range(12):
                            for aa in (ATh, ATl):
                                first = (s_ == 0 and aa is ATh)
                                kb.t(lambda e, o=pav_b.ap, w=vbh.ap[:, s_ * 128:(s_ + 1) * 128],
                                     r=aa.ap[:, s_ * 512:(s_ + 1) * 512], s0=first:
                                     e.matmul(o, w, r, start=s0, stop=False),
                                     reads=(vbh, ATh, ATl),
                                     wdep=[pav_b] if first else (),
                                     inc=False)
                        for s_ in range(12):
                            last = s_ == 11
                            kb.t(lambda e, o=pav_b.ap, w=vbl.ap[:, s_ * 128:(s_ + 1) * 128],
                                 r=ATh.ap[:, s_ * 512:(s_ + 1) * 512], s1=last:
                                 e.matmul(o, w, r, start=False, stop=s1),
                                 reads=(vbh, vbl, ATh, ATl),
                                 writes=[pav_b] if last else (),
                                 inc=last)
                        oh = oh_p[qh % 2]
                        ol = ol_p[qh % 2]
                        split_w(pav_b, oh, ol)
                        qs2 = slice(g2 * 512, (g2 + 1) * 512)
                        kb.g(oscr_h[qh * 128:(qh + 1) * 128, qs2], oh, 0 + qh % 2)
                        kb.g(oscr_l[qh * 128:(qh + 1) * 128, qs2], ol, 2 + qh % 2)
            kb.barrier()
            kb.flush()
        if upto <= 3:
            return nc

        with ExitStack() as ps:
            sb = lambda n, shp, dt: ps.enter_context(nc.sbuf_tensor(n, shp, dt))[:]
            oT_h = bufs_of(sb("oTh", [128, NRC * 512], BF16), NRC, 512)
            oT_l = bufs_of(sb("oTl", [128, NRC * 512], BF16), NRC, 512)
            wbuf_p = [Buf(sb(f"wb3{i}", [128, 1024], F32)) for i in range(2)]
            wh_p = [Buf(sb(f"wh3{i}", [128, 1024], BF16)) for i in range(2)]
            wl_p = [Buf(sb(f"wl3{i}", [128, 1024], BF16)) for i in range(2)]
            sf = Buf(sb("sf3", [128, 512], F32))
            pb = [Buf(ps.enter_context(nc.psum_tensor(f"pb3{i}", [128, 512], F32))[:])
                  for i in range(8)]
            for tcq in range(2):
                ts_ = slice(tcq * 512, (tcq + 1) * 512)
                for oc in range(NRC):
                    kb.y(oT_h[oc], oscr_h[oc * 128:(oc + 1) * 128, ts_], 4)
                    kb.y(oT_l[oc], oscr_l[oc * 128:(oc + 1) * 128, ts_], 4)
                ld_all = ("y4", kb.cnt["y4"])
                for b in oT_h + oT_l:
                    b.wrs = dict([ld_all])
                for rnd in range(4):
                    for oc in range(NRC):
                        wb, wh, wl = wbuf_p[oc % 2], wh_p[oc % 2], wl_p[oc % 2]
                        kb.y(wb, wo[oc * 128:(oc + 1) * 128, rnd * 1024:(rnd + 1) * 1024],
                             2 + oc % 2)
                        split_w(wb, wh, wl)
                        first = oc == 0
                        last = oc == NRC - 1
                        for dch in range(8):
                            p = pb[dch]
                            for j, xs in enumerate((oT_h[oc], oT_l[oc])):
                                kb.t(lambda e, o=p.ap, w=wh.ap[:, dch * 128:(dch + 1) * 128],
                                     r=xs.ap, s0=(first and j == 0):
                                     e.matmul(o, w, r, start=s0, stop=False),
                                     reads=(wh, oT_h[oc], oT_l[oc]),
                                     wdep=[p] if (first and j == 0) else (),
                                     inc=False)
                        for dch in range(8):
                            p = pb[dch]
                            kb.t(lambda e, o=p.ap, w=wl.ap[:, dch * 128:(dch + 1) * 128],
                                 r=oT_h[oc].ap, s1=last:
                                 e.matmul(o, w, r, start=False, stop=s1),
                                 reads=(wh, wl, oT_h[oc], oT_l[oc]),
                                 writes=[p] if last else (),
                                 inc=True)
                    for dch in range(8):
                        p = pb[dch]
                        dp = rnd * 8 + dch
                        kb.v(lambda e, o=sf.ap, i=p.ap: e.tensor_copy(o, i),
                             reads=[p], writes=[sf])
                        kb.g(sscr[dp * 128:(dp + 1) * 128, ts_], sf, 0)
            kb.barrier()
            kb.flush()
        if upto <= 4:
            return nc

        with ExitStack() as ps:
            sb = lambda n, shp, dt: ps.enter_context(nc.sbuf_tensor(n, shp, dt))[:]
            for nm, (zt, st_, oh_, ol_, n_, w_) in enumerate((
                    (zg, sg, nzsg_h, nzsg_l, NRC, FF),
                    (zu, su, nzsu_h, nzsu_l, NRC, FF),
                    (zd, sd, nzsd_h, nzsd_l, NCC, D))):
                half = w_ // 2
                zb = Buf(sb(f"zb{nm}", [n_, half], U8))
                sbf = Buf(sb(f"sb{nm}", [n_, half], F32))
                tb = Buf(sb(f"tb{nm}", [n_, half], F32))
                hb = Buf(sb(f"hb{nm}", [n_, half], BF16))
                lb = Buf(sb(f"lb{nm}", [n_, half], BF16))
                for hf in range(2):
                    cs = slice(hf * half, (hf + 1) * half)
                    kb.y(zb, zt[:, cs], nm * 2)
                    kb.y(sbf, st_[:, cs], nm * 2 + 1)
                    kb.v(lambda e, o=tb.ap, a=zb.ap, b=sbf.ap:
                         e.tensor_tensor(o, a, b, ALU.mult),
                         reads=[zb, sbf], writes=[tb])
                    kb.s(lambda e, o=hb.ap, i=tb.ap:
                         e.activation(o, i, AF.Copy, scale=-1.0),
                         reads=[tb], writes=[hb])
                    kb.v(lambda e, o=lb.ap, i=tb.ap, hh=hb.ap:
                         e.scalar_tensor_tensor(o, i, -1.0, hh, ALU.mult, ALU.subtract),
                         reads=[tb, hb], writes=[lb])
                    kb.g(oh_[:, cs], hb, nm * 2)
                    kb.g(ol_[:, cs], lb, nm * 2 + 1)
            kb.barrier()
            kb.flush()

        with ExitStack() as ps:
            sb = lambda n, shp, dt: ps.enter_context(nc.sbuf_tensor(n, shp, dt))[:]
            xnT_h = bufs_of(sb("xnT_h4", [128, NRC * 1024], BF16), NRC, 1024)
            xnT_l = bufs_of(sb("xnT_l4", [128, NRC * 1024], BF16), NRC, 1024)
            for rc in range(NRC):
                kb.y(xnT_h[rc], xnscr_h[rc * 128:(rc + 1) * 128, :], 4)
                kb.y(xnT_l[rc], xnscr_l[rc * 128:(rc + 1) * 128, :], 4)
            gs32 = Buf(sb("gs32", [128, 1024], BF16))
            kb.y(gs32, gsel32[:, :], 4)
            ld_all = ("y4", kb.cnt["y4"])
            for b in xnT_h + xnT_l + [gs32]:
                b.wrs = dict([ld_all])
            Rf = Buf(sb("Rf", [32, 1024], F32))
            Rbh = Buf(sb("Rbh", [32, 1024], BF16))
            Rbl = Buf(sb("Rbl", [32, 1024], BF16))
            with ExitStack() as prs:
                pR = Buf(prs.enter_context(nc.psum_tensor("pR", [32, 512], F32))[:])
                for tcq in range(2):
                    n = 0
                    for rc in range(NRC):
                        for xs in (xnT_h[rc], xnT_l[rc]):
                            first = n == 0
                            last = n == 2 * NRC - 1
                            kb.t(lambda e, o=pR.ap, w=gs32.ap[:, rc * 32:(rc + 1) * 32],
                                 r=xs.ap[:, tcq * 512:(tcq + 1) * 512], s0=first, s1=last:
                                 e.matmul(o, w, r, start=s0, stop=s1),
                                 reads=(gs32, xnT_h[rc], xnT_l[rc]),
                                 writes=[pR] if last else (),
                                 wdep=[pR] if first else (),
                                 inc=last)
                            n += 1
                    kb.v(lambda e, o=Rf.ap[:, tcq * 512:(tcq + 1) * 512], i=pR.ap:
                         e.tensor_copy(o, i), reads=[pR], wpart=[Rf])
            split_w(Rf, Rbh, Rbl)

            qw_p = [Buf(sb(f"qw4{i}", [128, 256], BF16)) for i in range(4)]
            sr_p = [Buf(sb(f"sr4{i}", [128, 256], F32)) for i in range(4)]
            wpf_p = [Buf(sb(f"wpf{i}", [128, 256], F32)) for i in range(4)]
            wph_p = [Buf(sb(f"wph{i}", [128, 256], BF16)) for i in range(4)]
            wpl_p = [Buf(sb(f"wpl{i}", [128, 256], BF16)) for i in range(4)]
            nz_h = [Buf(sb(f"nzh{i}", [32, 256], BF16)) for i in range(2)]
            nz_l = [Buf(sb(f"nzl{i}", [32, 256], BF16)) for i in range(2)]
            gsl = Buf(sb("gsl", [128, 512], F32))
            uf = Buf(sb("uf", [128, 512], F32))
            hf = Buf(sb("hf", [128, 512], F32))
            hh = Buf(sb("hh", [128, 512], BF16))
            hl = Buf(sb("hl", [128, 512], BF16))
            pb = [Buf(ps.enter_context(nc.psum_tensor(f"pb4{i}", [128, 512], F32))[:])
                  for i in range(8)]
            nzscr = ((nzsg_h, nzsg_l), (nzsu_h, nzsu_l))

            for cb in range(43):
                cs = slice(cb * 256, (cb + 1) * 256)
                for mi in range(2):
                    kb.y(nz_h[mi], nzscr[mi][0][:, cs], 5 + mi)
                    kb.y(nz_l[mi], nzscr[mi][1][:, cs], 7 + mi)
                for mi in range(2):
                    for cc in range(2):
                        for tcq in range(2):
                            p = pb[mi * 4 + cc * 2 + tcq]
                            ops = [(nz_h[mi], Rbh, True), (nz_h[mi], Rbl, False),
                                   (nz_l[mi], Rbh, False)]
                            for i, (zz, rr, st0) in enumerate(ops):
                                kb.t(lambda e, o=p.ap, w=zz.ap[:, cc * 128:(cc + 1) * 128],
                                     r=rr.ap[:, tcq * 512:(tcq + 1) * 512], s0=st0:
                                     e.matmul(o, w, r, start=s0, stop=False),
                                     reads=(nz_h[mi], nz_l[mi], Rbh, Rbl),
                                     wdep=[p] if i == 0 else (),
                                     inc=(i == 2))
                for rc in range(NRC):
                    for mi, (qwt, sst) in enumerate(((qwg, sg), (qwu, su))):
                        pi = (rc % 2) * 2 + mi
                        qwb, srb = qw_p[pi], sr_p[pi]
                        wpf, wph, wpl = wpf_p[pi], wph_p[pi], wpl_p[pi]
                        kb.g(qwb, qwt[rc * 128:(rc + 1) * 128, cs], 2 + pi)
                        kb.g(srb, bass.AP(sst.tensor, rc * FF + cb * 256, [[0, 128], [1, 256]]),
                             8 + pi)
                        kb.v(lambda e, o=wpf.ap, a=qwb.ap, b=srb.ap:
                             e.tensor_tensor(o, a, b, ALU.mult),
                             reads=[qwb, srb], writes=[wpf])
                        split_w(wpf, wph, wpl)
                        last = rc == NRC - 1
                        for cc in range(2):
                            for xs, tcq in ((xnT_h[rc], 0), (xnT_h[rc], 1),
                                            (xnT_l[rc], 0), (xnT_l[rc], 1)):
                                p = pb[mi * 4 + cc * 2 + tcq]
                                kb.t(lambda e, o=p.ap,
                                     w=wph.ap[:, cc * 128:(cc + 1) * 128],
                                     r=xs.ap[:, tcq * 512:(tcq + 1) * 512]:
                                     e.matmul(o, w, r, start=False, stop=False),
                                     reads=(wph, xnT_h[rc], xnT_l[rc]),
                                     inc=False)
                        for cc in range(2):
                            for tcq in range(2):
                                p = pb[mi * 4 + cc * 2 + tcq]
                                kb.t(lambda e, o=p.ap,
                                     w=wpl.ap[:, cc * 128:(cc + 1) * 128],
                                     r=xnT_h[rc].ap[:, tcq * 512:(tcq + 1) * 512], s1=last:
                                     e.matmul(o, w, r, start=False, stop=s1),
                                     reads=(wph, wpl, xnT_h[rc], xnT_l[rc]),
                                     writes=[p] if last else (),
                                     inc=True)
                for cc in range(2):
                    ccg = cb * 2 + cc
                    for tcq in range(2):
                        pg = pb[0 * 4 + cc * 2 + tcq]
                        pu = pb[1 * 4 + cc * 2 + tcq]
                        kb.s(lambda e, o=gsl.ap, i=pg.ap: e.activation(o, i, AF.Silu),
                             reads=[pg], writes=[gsl])
                        kb.v(lambda e, o=uf.ap, i=pu.ap: e.tensor_copy(o, i),
                             reads=[pu], writes=[uf])
                        kb.v(lambda e, o=hf.ap, a=gsl.ap, b=uf.ap:
                             e.tensor_tensor(o, a, b, ALU.mult),
                             reads=[gsl, uf], writes=[hf])
                        split_w(hf, hh, hl)
                        ts_ = slice(tcq * 512, (tcq + 1) * 512)
                        kb.g(hscr_h[ccg * 128:(ccg + 1) * 128, ts_], hh, 0)
                        kb.g(hscr_l[ccg * 128:(ccg + 1) * 128, ts_], hl, 1)
            kb.barrier()
            kb.flush()

        with ExitStack() as ps:
            sb = lambda n, shp, dt: ps.enter_context(nc.sbuf_tensor(n, shp, dt))[:]
            gs86g = Buf(sb("gs86g", [128, 86 * 86], BF16))
            kb.y(gs86g, gsel86[:, :], 4)
            h_p = [Buf(sb(f"hp{i}", [128, 512], BF16)) for i in range(4)]
            Rhf_g = Buf(sb("Rhfg", [86, 1024], F32))
            pR5 = Buf(ps.enter_context(nc.psum_tensor("pR5", [86, 512], F32))[:])
            for tcq in range(2):
                ts_ = slice(tcq * 512, (tcq + 1) * 512)
                n = 0
                for cc in range(NCC):
                    hb_ = h_p[cc % 2]
                    lb_ = h_p[2 + cc % 2]
                    kb.y(hb_, hscr_h[cc * 128:(cc + 1) * 128, ts_], cc % 2)
                    kb.y(lb_, hscr_l[cc * 128:(cc + 1) * 128, ts_], 2 + cc % 2)
                    for xs in (hb_, lb_):
                        first = n == 0
                        last = n == 2 * NCC - 1
                        kb.t(lambda e, o=pR5.ap, w=gs86g.ap[:, cc * 86:(cc + 1) * 86],
                             r=xs.ap, s0=first, s1=last:
                             e.matmul(o, w, r, start=s0, stop=s1),
                             reads=(gs86g, hb_, lb_),
                             writes=[pR5] if last else (),
                             wdep=[pR5] if first else (),
                             inc=(n % 2 == 1))
                        n += 1
                kb.v(lambda e, o=Rhf_g.ap[:, ts_], i=pR5.ap: e.tensor_copy(o, i),
                     reads=[pR5], wpart=[Rhf_g])
            Rhh_g = Buf(sb("Rhhg", [86, 1024], BF16))
            Rhl_g = Buf(sb("Rhlg", [86, 1024], BF16))
            split_w(Rhf_g, Rhh_g, Rhl_g)
            rh_scr_h = SCR("rh_scr_h", [86, 1024], BF16)
            rh_scr_l = SCR("rh_scr_l", [86, 1024], BF16)
            kb.g(rh_scr_h[:, :], Rhh_g, 0)
            kb.g(rh_scr_l[:, :], Rhl_g, 1)
            kb.barrier()
            kb.flush()

        with ExitStack() as ps:
            sb = lambda n, shp, dt: ps.enter_context(nc.sbuf_tensor(n, shp, dt))[:]
            Rhh = Buf(sb("Rhh", [86, 1024], BF16))
            Rhl = Buf(sb("Rhl", [86, 1024], BF16))
            kb.y(Rhh, rh_scr_h[:, :], 4)
            kb.y(Rhl, rh_scr_l[:, :], 4)
            ld_all = ("y4", kb.cnt["y4"])
            Rhh.wrs = dict([ld_all])
            Rhl.wrs = dict([ld_all])
            qw_p = [Buf(sb(f"qw5{i}", [128, 512], BF16)) for i in range(2)]
            sr_p = [Buf(sb(f"sr5{i}", [128, 512], F32)) for i in range(2)]
            wpf_p = [Buf(sb(f"wpf5{i}", [128, 512], F32)) for i in range(2)]
            wph_p = [Buf(sb(f"wph5{i}", [128, 512], BF16)) for i in range(2)]
            wpl_p = [Buf(sb(f"wpl5{i}", [128, 512], BF16)) for i in range(2)]
            hh_p = [Buf(sb(f"hh5{i}", [128, 1024], BF16)) for i in range(2)]
            hl_p = [Buf(sb(f"hl5{i}", [128, 1024], BF16)) for i in range(2)]
            nzd_h = Buf(sb("nzdh", [86, 512], BF16))
            nzd_l = Buf(sb("nzdl", [86, 512], BF16))
            gbuf_p = [Buf(sb(f"gbuf{i}", [128, 512], F32)) for i in range(2)]
            stb_p = [Buf(sb(f"stbuf{i}", [128, 512], F32)) for i in range(2)]
            sgb_p = [Buf(sb(f"sgbuf{i}", [128, 512], F32)) for i in range(2)]
            xb_p = [Buf(sb(f"xbuf{i}", [128, 128], F32)) for i in range(2)]
            ob_p = [Buf(sb(f"obuf{i}", [128, 128], F32)) for i in range(2)]
            pb = [Buf(ps.enter_context(nc.psum_tensor(f"pb5{i}", [128, 512], F32))[:])
                  for i in range(8)]

            for rnd in range(8):
                rs = slice(rnd * 512, (rnd + 1) * 512)
                kb.y(nzd_h, nzsd_h[:, rs], 5)
                kb.y(nzd_l, nzsd_l[:, rs], 6)
                for dch in range(4):
                    for tcq in range(2):
                        p = pb[dch * 2 + tcq]
                        ops = [(nzd_h, Rhh, True), (nzd_h, Rhl, False), (nzd_l, Rhh, False)]
                        for i, (zz, rr, st0) in enumerate(ops):
                            kb.t(lambda e, o=p.ap, w=zz.ap[:, dch * 128:(dch + 1) * 128],
                                 r=rr.ap[:, tcq * 512:(tcq + 1) * 512], s0=st0:
                                 e.matmul(o, w, r, start=s0, stop=False),
                                 reads=(nzd_h, nzd_l, Rhh, Rhl),
                                 wdep=[p] if i == 0 else (),
                                 inc=(i == 2))
                for cc in range(NCC):
                    qwb, srb = qw_p[cc % 2], sr_p[cc % 2]
                    wpf, wph, wpl = wpf_p[cc % 2], wph_p[cc % 2], wpl_p[cc % 2]
                    hhb, hlb = hh_p[cc % 2], hl_p[cc % 2]
                    kb.y(hhb, hscr_h[cc * 128:(cc + 1) * 128, :], 0 + cc % 2)
                    kb.y(hlb, hscr_l[cc * 128:(cc + 1) * 128, :], 2 + cc % 2)
                    kb.g(qwb, qwd[cc * 128:(cc + 1) * 128, rs], 2 + cc % 2)
                    kb.g(srb, bass.AP(sd.tensor, cc * D + rnd * 512, [[0, 128], [1, 512]]),
                         8 + cc % 2)
                    kb.v(lambda e, o=wpf.ap, a=qwb.ap, b=srb.ap:
                         e.tensor_tensor(o, a, b, ALU.mult),
                         reads=[qwb, srb], writes=[wpf])
                    split_w(wpf, wph, wpl)
                    last = cc == NCC - 1
                    for dch in range(4):
                        for xs, tcq in ((hhb, 0), (hhb, 1), (hlb, 0), (hlb, 1)):
                            p = pb[dch * 2 + tcq]
                            kb.t(lambda e, o=p.ap, w=wph.ap[:, dch * 128:(dch + 1) * 128],
                                 r=xs.ap[:, tcq * 512:(tcq + 1) * 512]:
                                 e.matmul(o, w, r, start=False, stop=False),
                                 reads=(wph, hhb, hlb),
                                 inc=False)
                    for dch in range(4):
                        for tcq in range(2):
                            p = pb[dch * 2 + tcq]
                            kb.t(lambda e, o=p.ap, w=wpl.ap[:, dch * 128:(dch + 1) * 128],
                                 r=hhb.ap[:, tcq * 512:(tcq + 1) * 512], s1=last:
                                 e.matmul(o, w, r, start=False, stop=s1),
                                 reads=(wph, wpl, hhb, hlb),
                                 writes=[p] if last else (),
                                 inc=True)
                for dch in range(4):
                    dp = rnd * 4 + dch
                    for tcq in range(2):
                        p = pb[dch * 2 + tcq]
                        ts_ = slice(tcq * 512, (tcq + 1) * 512)
                        par = (dch * 2 + tcq) % 2
                        gbuf, stbuf, sgbuf = gbuf_p[par], stb_p[par], sgb_p[par]
                        kb.s(lambda e, o=gbuf.ap, i=p.ap: e.activation(o, i, AF.Silu),
                             reads=[p], writes=[gbuf])
                        kb.y(stbuf, sscr[dp * 128:(dp + 1) * 128, ts_], 10 + par)
                        kb.v(lambda e, o=sgbuf.ap, a=gbuf.ap, b=stbuf.ap:
                             e.tensor_tensor(o, a, b, ALU.mult),
                             reads=[gbuf, stbuf], writes=[sgbuf])
                        for tt in range(4):
                            xpar = tt % 2
                            xbuf, obuf = xb_p[xpar], ob_p[xpar]
                            kb.t(lambda e, o=p.ap[:, 0:128],
                                 i=sgbuf.ap[:, tt * 128:(tt + 1) * 128], ident=idf.ap:
                                 e.transpose(o, i, ident),
                                 reads=[sgbuf, idf], writes=[p])
                            rows = slice(1024 + tcq * 512 + tt * 128,
                                         1024 + tcq * 512 + (tt + 1) * 128)
                            cols = slice(dp * 128, (dp + 1) * 128)
                            kb.y(xbuf, xkv[rows, cols], 12 + xpar)
                            kb.v(lambda e, o=obuf.ap, a=xbuf.ap, b=p.ap[:, 0:128]:
                                 e.tensor_tensor(o, a, b, ALU.add),
                                 reads=[xbuf, p], writes=[obuf])
                            orows = slice(tcq * 512 + tt * 128, tcq * 512 + (tt + 1) * 128)
                            kb.g(out[orows, cols], obuf, 4 + xpar)
            kb.barrier()
            kb.flush()

    return nc


def _host_inputs(L0_attn_x, L0_attn_wq, L0_attn_wk, L0_attn_wv, L0_attn_wo,
                 L0_ffn_qw_gate, L0_ffn_z_gate, L0_ffn_s_gate,
                 L0_ffn_qw_up, L0_ffn_z_up, L0_ffn_s_up,
                 L0_ffn_qw_down, L0_ffn_z_down, L0_ffn_s_down):
    x = np.asarray(L0_attn_x, dtype=np.float32)
    shared = {
        "wq": np.ascontiguousarray(np.asarray(L0_attn_wq, np.float32)),
        "wk": np.ascontiguousarray(np.asarray(L0_attn_wk, np.float32)),
        "wv": np.ascontiguousarray(np.asarray(L0_attn_wv, np.float32)),
        "wo": np.ascontiguousarray(np.asarray(L0_attn_wo, np.float32)),
        "qwg": np.ascontiguousarray(np.asarray(L0_ffn_qw_gate).astype(np.uint8)),
        "qwu": np.ascontiguousarray(np.asarray(L0_ffn_qw_up).astype(np.uint8)),
        "qwd": np.ascontiguousarray(np.asarray(L0_ffn_qw_down).astype(np.uint8)),
        "zg": np.ascontiguousarray(np.asarray(L0_ffn_z_gate).astype(np.uint8)),
        "zu": np.ascontiguousarray(np.asarray(L0_ffn_z_up).astype(np.uint8)),
        "zd": np.ascontiguousarray(np.asarray(L0_ffn_z_down).astype(np.uint8)),
        "sg": np.ascontiguousarray(np.asarray(L0_ffn_s_gate, np.float32)),
        "su": np.ascontiguousarray(np.asarray(L0_ffn_s_up, np.float32)),
        "sd": np.ascontiguousarray(np.asarray(L0_ffn_s_down, np.float32)),
        "gsel32": _host_gsel(32),
        "gsel86": _host_gsel(86),
        "identf": np.eye(128, dtype=np.float32),
        "identb": np.eye(128, dtype=BF16NP),
    }
    per_h = {}
    for h in (0, 1):
        Ch, Cl, Sh, Sl = _host_rope(h)
        per_h[h] = {"masks": _host_masks(h),
                    "ropeCh": Ch, "ropeCl": Cl, "ropeSh": Sh, "ropeSl": Sl}
    in_maps = []
    for c in range(8):
        b, h = c // 2, c % 2
        if h == 1:
            xkv_a = np.ascontiguousarray(x[b])
        else:
            xkv_a = np.concatenate([np.zeros((1024, D), np.float32), x[b, :1024]], axis=0)
        m = {"xkv": xkv_a}
        m.update(shared)
        m.update(per_h[h])
        in_maps.append(m)
    return in_maps


def kernel(**inputs):
    in_maps = _host_inputs(**inputs)
    nc = build_program(upto=5, dbg=False)
    res = run_bass_kernel_spmd(nc, in_maps, core_ids=list(range(8)))
    outp = np.empty((4, 2048, D), np.float32)
    for c in range(8):
        b, h = c // 2, c % 2
        outp[b, h * 1024:(h + 1) * 1024, :] = res.results[c]["out"]
    return outp


# version 06 (final)
# speedup vs baseline: 1.0137x; 1.0137x over previous
import numpy as np
import ml_dtypes
from contextlib import ExitStack
import concourse.bass as bass
import concourse.mybir as mybir
from concourse.bass_utils import run_bass_kernel_spmd

F32 = mybir.dt.float32
BF16 = mybir.dt.bfloat16
U8 = mybir.dt.uint8
AX = mybir.AxisListType.X
AF = mybir.ActivationFunctionType
ALU = mybir.AluOpType
BF16NP = ml_dtypes.bfloat16

D = 4096
FF = 11008
HQ = 32
HKV = 8
HD = 128
NRC = D // 128
NCC = FF // 128
TSRC = 2048
TOWN = 1024
NW = 9
SM = float(np.float32(HD ** -0.5))
EPS = 1e-5
NEG = -3.0e38
NY = 15
NG = 12


def _merge(d, tok):
    s, v = tok
    if d.get(s, 0) < v:
        d[s] = v


class Buf:
    __slots__ = ("ap", "wrs", "rds")

    def __init__(self, ap):
        self.ap = ap
        self.wrs = {}
        self.rds = {}


class KB:
    STREAMS = ("y", "g", "t", "v", "s")

    def __init__(self, nc, sems):
        self.nc = nc
        self.sems = sems
        self.cnt = {k: 0 for k in sems}
        self.steps = {k: [] for k in self.STREAMS}
        self.sat = {k: {} for k in self.STREAMS}

    def _emit(self, stream, fn, deps, inc):
        waits = []
        sat = self.sat[stream]
        for s, v in deps.items():
            if sat.get(s, 0) >= v:
                continue
            waits.append((s, v))
            sat[s] = v
        tok = None
        if inc is not None:
            sem, n = inc
            self.cnt[sem] += n
            tok = (sem, self.cnt[sem])
        self.steps[stream].append((fn, waits, inc))
        return tok

    def _op(self, stream, sem, fn, reads, writes, wpart, wdep, deps, inc):
        dl = {}
        for t in deps:
            if t is not None:
                _merge(dl, t)
        for b in reads:
            for kv in b.wrs.items():
                _merge(dl, kv)
        for b in list(writes) + list(wpart) + list(wdep):
            for kv in b.wrs.items():
                _merge(dl, kv)
            for kv in b.rds.items():
                _merge(dl, kv)
        tok = self._emit(stream, fn, dl, (sem, 1) if inc else None)
        if tok is not None:
            for b in reads:
                _merge(b.rds, tok)
            for b in writes:
                b.wrs = dict([tok])
                b.rds = {}
            for b in wpart:
                _merge(b.wrs, tok)
        return tok

    def t(self, fn, reads=(), writes=(), wpart=(), wdep=(), deps=(), inc=True):
        return self._op("t", "t", fn, reads, writes, wpart, wdep, deps, inc)

    def v(self, fn, reads=(), writes=(), wpart=(), wdep=(), deps=(), inc=True):
        return self._op("v", "v", fn, reads, writes, wpart, wdep, deps, inc)

    def s(self, fn, reads=(), writes=(), wpart=(), wdep=(), deps=(), inc=True):
        return self._op("s", "s", fn, reads, writes, wpart, wdep, deps, inc)

    def gc(self, fn, reads=(), writes=(), wpart=(), wdep=(), deps=(), inc=True):
        return self._op("g", "gc", fn, reads, writes, wpart, wdep, deps, inc)

    def _dma(self, stream, ch, dst, src, deps, wpart, dst_ap=None):
        sem = f"{stream}{ch}"
        dl = {}
        for t in deps:
            if t is not None:
                _merge(dl, t)
        reads, writes, wparts = [], [], []
        if isinstance(src, Buf):
            reads.append(src)
            src_ap = src.ap
        else:
            src_ap = src
        if isinstance(dst, Buf):
            (wparts if wpart else writes).append(dst)
            if dst_ap is None:
                dst_ap = dst.ap
        else:
            dst_ap = dst
        for b in reads:
            for kv in b.wrs.items():
                _merge(dl, kv)
        for b in writes + wparts:
            for kv in b.wrs.items():
                _merge(dl, kv)
            for kv in b.rds.items():
                _merge(dl, kv)
        fn = lambda e, o=dst_ap, i=src_ap: e.dma_start(out=o, in_=i)
        tok = self._emit(stream, fn, dl, (sem, 16))
        for b in reads:
            _merge(b.rds, tok)
        for b in writes:
            b.wrs = dict([tok])
            b.rds = {}
        for b in wparts:
            _merge(b.wrs, tok)
        return tok

    def y(self, dst, src, ch, deps=(), wpart=False, dst_ap=None):
        return self._dma("y", ch, dst, src, deps, wpart, dst_ap)

    def g(self, dst, src, ch, deps=(), wpart=False, dst_ap=None):
        return self._dma("g", ch, dst, src, deps, wpart, dst_ap)

    def barrier(self):
        for st, sem in (("t", "t"), ("v", "v"), ("s", "s"), ("g", "gc")):
            self._op(st, sem, lambda e: e.nop(), (), (), (), (),
                     [(sem, self.cnt[sem])], True)
        tgt = {k: v for k, v in self.cnt.items() if v > 0}
        for st in self.STREAMS:
            self._emit(st, None, dict(tgt), None)
            self.sat[st] = dict(tgt)

    def flush(self):
        nc = self.nc
        sems = self.sems
        steps = self.steps
        self.steps = {k: [] for k in self.STREAMS}

        def mk(stream):
            def body(eng):
                for fn, waits, inc in steps[stream]:
                    for (s, v) in waits:
                        eng.wait_ge(sems[s], v)
                    if fn is None:
                        continue
                    ins = fn(eng)
                    if inc is not None:
                        ins.then_inc(sems[inc[0]], inc[1])
            return body

        with nc.Block() as blk:
            blk.sync(mk("y"))
            blk.gpsimd(mk("g"))
            blk.tensor(mk("t"))
            blk.vector(mk("v"))
            blk.scalar(mk("s"))


def bufs_of(sb, n, width):
    return [Buf(sb[:, i * width:(i + 1) * width]) for i in range(n)]


def _host_masks(h):
    m = np.zeros((8, 128, NW * 128), np.float32)
    ii = np.arange(128)[:, None]
    jj = np.arange(128)[None, :]
    UT = np.where(jj > ii, np.float32(0), np.float32(NEG))
    LT = np.where(jj <= ii, np.float32(0), np.float32(NEG))
    for l in range(8):
        for w in range(NW):
            blk = m[l][:, w * 128:(w + 1) * 128]
            if w == 8:
                blk[:] = LT
            elif h == 1:
                blk[:] = UT if w == 0 else 0.0
            else:
                blk[:] = NEG if w < 8 - l else 0.0
    return np.ascontiguousarray(m.reshape(8 * 128, NW * 128))


def _host_rope(h):
    inv = np.power(10000.0, -np.arange(64) / 64.0)
    pos = (h * 1024 - 1024) + np.arange(2048, dtype=np.float64)
    ang = pos[:, None] * inv[None, :]
    c = np.cos(ang).T
    s = np.sin(ang).T
    C = np.vstack([c, c])
    S = np.vstack([-s, s])
    Ch = np.ascontiguousarray(C.astype(np.float32))
    Cl = np.ascontiguousarray((C - Ch).astype(BF16NP))
    Sh = np.ascontiguousarray(S.astype(np.float32))
    Sl = np.ascontiguousarray((S - Sh).astype(BF16NP))
    return Ch, Cl, Sh, Sl


def _host_gsel(n):
    g = np.zeros((128, n * n), BF16NP)
    for i in range(n):
        g[:, i * n + i] = 1
    return g


def build_program(upto=5, dbg=False):
    nc = bass.Bass("TRN2", target_bir_lowering=False)

    def DI(name, shape, dt):
        return nc.dram_tensor(name, shape, dt, kind="ExternalInput").ap()

    xkv = DI("xkv", [TSRC, D], F32)
    wq = DI("wq", [D, D], F32)
    wk = DI("wk", [D, HKV * HD], F32)
    wv = DI("wv", [D, HKV * HD], F32)
    wo = DI("wo", [D, D], F32)
    qwg = DI("qwg", [D, FF], U8)
    qwu = DI("qwu", [D, FF], U8)
    qwd = DI("qwd", [FF, D], U8)
    zg = DI("zg", [NRC, FF], U8)
    zu = DI("zu", [NRC, FF], U8)
    zd = DI("zd", [NCC, D], U8)
    sg = DI("sg", [NRC, FF], F32)
    su = DI("su", [NRC, FF], F32)
    sd = DI("sd", [NCC, D], F32)
    masks = DI("masks", [8 * 128, NW * 128], F32)
    ropeCh = DI("ropeCh", [128, TSRC], F32)
    ropeCl = DI("ropeCl", [128, TSRC], BF16)
    ropeSh = DI("ropeSh", [128, TSRC], F32)
    ropeSl = DI("ropeSl", [128, TSRC], BF16)
    gsel32 = DI("gsel32", [128, 32 * 32], BF16)
    gsel86 = DI("gsel86", [128, 86 * 86], BF16)
    identf = DI("identf", [128, 128], F32)
    identb = DI("identb", [128, 128], BF16)

    out = nc.dram_tensor("out", [TOWN, D], F32, kind="ExternalOutput").ap()

    skind = dict(kind="ExternalOutput") if dbg else {}

    def SCR(name, shape, dt):
        return nc.dram_tensor(name, shape, dt, **skind).ap()

    kscr_h = SCR("kscr_h", [HKV * 128, TSRC], BF16)
    kscr_l = SCR("kscr_l", [HKV * 128, TSRC], BF16)
    vscr_h = SCR("vscr_h", [16 * 128, HKV * HD], BF16)
    vscr_l = SCR("vscr_l", [16 * 128, HKV * HD], BF16)
    xnscr_h = SCR("xnscr_h", [NRC * 128, TOWN], BF16)
    xnscr_l = SCR("xnscr_l", [NRC * 128, TOWN], BF16)
    qscr_h = SCR("qscr_h", [HQ * 128, TOWN], BF16)
    qscr_l = SCR("qscr_l", [HQ * 128, TOWN], BF16)
    oscr_h = SCR("oscr_h", [HQ * 128, TOWN], BF16)
    oscr_l = SCR("oscr_l", [HQ * 128, TOWN], BF16)
    sscr = SCR("sscr", [NRC * 128, TOWN], F32)
    hscr_h = SCR("hscr_h", [NCC * 128, TOWN], BF16)
    hscr_l = SCR("hscr_l", [NCC * 128, TOWN], BF16)
    nzsg_h = SCR("nzsg_h", [NRC, FF], BF16)
    nzsg_l = SCR("nzsg_l", [NRC, FF], BF16)
    nzsu_h = SCR("nzsu_h", [NRC, FF], BF16)
    nzsu_l = SCR("nzsu_l", [NRC, FF], BF16)
    nzsd_h = SCR("nzsd_h", [NCC, D], BF16)
    nzsd_l = SCR("nzsd_l", [NCC, D], BF16)

    with ExitStack() as es:
        sems = {}
        for name in (["t", "v", "s", "gc"] + [f"y{i}" for i in range(NY)]
                     + [f"g{i}" for i in range(NG)]):
            sems[name] = es.enter_context(nc.semaphore("sem_" + name))
        kb = KB(nc, sems)

        idf = Buf(es.enter_context(nc.sbuf_tensor("idf", [128, 128], F32))[:])
        idb = Buf(es.enter_context(nc.sbuf_tensor("idb", [128, 128], BF16))[:])
        epsb = Buf(es.enter_context(nc.sbuf_tensor("epsb", [128, 1], F32))[:])
        kb.y(idf, identf[:, :], 14)
        kb.y(idb, identb[:, :], 14)
        idf.wrs = {"y14": 32}
        idb.wrs = {"y14": 32}
        kb.gc(lambda e, o=epsb.ap: e.memset(o, EPS), writes=[epsb])

        def split_w(wb, wh, wl):
            kb.s(lambda e, o=wh.ap, i=wb.ap: e.copy(o, i), reads=[wb], writes=[wh])
            kb.v(lambda e, o=wl.ap, i=wb.ap, hh=wh.ap:
                 e.scalar_tensor_tensor(o, i, 1.0, hh, ALU.mult, ALU.subtract),
                 reads=[wb, wh], writes=[wl])

        def rope(psrc, cs, rCh, rCl, rSh, rSl, kf, rot, t1, t2, t3, oh_, ol_):
            kb.v(lambda e, o=kf.ap, i=psrc.ap: e.tensor_copy(o, i),
                 reads=[psrc], writes=[kf])
            kb.v(lambda e, o=rot.ap[0:64, :], i=kf.ap[64:128, :]: e.tensor_copy(o, i),
                 reads=[kf], wpart=[rot])
            kb.v(lambda e, o=rot.ap[64:128, :], i=kf.ap[0:64, :]: e.tensor_copy(o, i),
                 reads=[kf], wpart=[rot])
            kb.v(lambda e, o=t1.ap, a=kf.ap, b=rCh.ap[:, cs]:
                 e.tensor_tensor(o, a, b, ALU.mult), reads=[kf, rCh], writes=[t1])
            kb.v(lambda e, o=t2.ap, a=rot.ap, b=rSh.ap[:, cs]:
                 e.tensor_tensor(o, a, b, ALU.mult), reads=[rot, rSh], writes=[t2])
            kb.v(lambda e, o=t3.ap, a=t1.ap, b=t2.ap:
                 e.tensor_tensor(o, a, b, ALU.add), reads=[t1, t2], writes=[t3])
            kb.v(lambda e, o=t1.ap, a=kf.ap, b=rCl.ap[:, cs]:
                 e.tensor_tensor(o, a, b, ALU.mult), reads=[kf, rCl], writes=[t1])
            kb.v(lambda e, o=t2.ap, a=rot.ap, b=rSl.ap[:, cs]:
                 e.tensor_tensor(o, a, b, ALU.mult), reads=[rot, rSl], writes=[t2])
            kb.v(lambda e, o=t1.ap, a=t1.ap, b=t2.ap:
                 e.tensor_tensor(o, a, b, ALU.add), reads=[t1, t2], writes=[t1])
            kb.v(lambda e, o=kf.ap, a=t3.ap, b=t1.ap:
                 e.tensor_tensor(o, a, b, ALU.add), reads=[t3, t1], writes=[kf])
            split_w(kf, oh_, ol_)

        with ExitStack() as ps:
            sb = lambda n, shp, dt: ps.enter_context(nc.sbuf_tensor(n, shp, dt))[:]
            rCh = Buf(sb("rCh", [128, TSRC], F32))
            rCl = Buf(sb("rCl", [128, TSRC], BF16))
            rSh = Buf(sb("rSh", [128, TSRC], F32))
            rSl = Buf(sb("rSl", [128, TSRC], BF16))
            kb.y(rCh, ropeCh[:, :], 4)
            kb.y(rCl, ropeCl[:, :], 4)
            kb.y(rSh, ropeSh[:, :], 4)
            kb.y(rSl, ropeSl[:, :], 4)
            ld_all = ("y4", kb.cnt["y4"])
            for b in (rCh, rCl, rSh, rSl):
                b.wrs = dict([ld_all])
            xt_p = [Buf(sb(f"xt{i}", [128, D], F32)) for i in range(2)]
            sq_p = [Buf(sb("sq", [128, D], F32))] * 2
            st1_p = [Buf(sb(f"st1{i}", [128, 1], F32)) for i in range(2)]
            stss_p = [Buf(sb(f"stss{i}", [128, 1], F32)) for i in range(2)]
            stm_p = [Buf(sb(f"stm{i}", [128, 1], F32)) for i in range(2)]
            stm2_p = [Buf(sb(f"stm2{i}", [128, 1], F32)) for i in range(2)]
            stvar_p = [Buf(sb(f"stvar{i}", [128, 1], F32)) for i in range(2)]
            stsd_p = [Buf(sb(f"stsd{i}", [128, 1], F32)) for i in range(2)]
            strr_p = [Buf(sb(f"str_{i}", [128, 1], F32)) for i in range(2)]
            stnb_p = [Buf(sb(f"stnb{i}", [128, 1], F32)) for i in range(2)]
            xnf_p = [Buf(sb(f"xnf{i}", [128, D], F32)) for i in range(2)]
            xnT_h = bufs_of(sb("xnT_h", [128, NRC * 512], BF16), NRC, 512)
            xnT_l = bufs_of(sb("xnT_l", [128, NRC * 512], BF16), NRC, 512)
            wbuf_p = [Buf(sb(f"wb{i}", [128, 1024], F32)) for i in range(2)]
            wh_p = [Buf(sb(f"whp{i}", [128, 1024], BF16)) for i in range(2)]
            wl_p = [Buf(sb(f"wlp{i}", [128, 1024], BF16)) for i in range(2)]
            kf = Buf(sb("kf", [128, 512], F32))
            rot = Buf(sb("rot", [128, 512], F32))
            t1 = Buf(sb("t1_", [128, 512], F32))
            t2 = Buf(sb("t2_", [128, 512], F32))
            t3 = Buf(sb("t3_", [128, 512], F32))
            kh = Buf(sb("kh", [128, 512], BF16))
            kl = Buf(sb("kl", [128, 512], BF16))
            vh = Buf(sb("vh", [128, 512], BF16))
            vl = Buf(sb("vl", [128, 512], BF16))
            pb = [Buf(ps.enter_context(nc.psum_tensor(f"pb{i}", [128, 512], F32))[:])
                  for i in range(8)]

            for tc in range(4):
                for tt in range(4):
                    xt = xt_p[tt % 2]
                    sq, st1, stss = sq_p[tt % 2], st1_p[tt % 2], stss_p[tt % 2]
                    stm, stm2, stvar = stm_p[tt % 2], stm2_p[tt % 2], stvar_p[tt % 2]
                    stsd, str_, stnb = stsd_p[tt % 2], strr_p[tt % 2], stnb_p[tt % 2]
                    xnf = xnf_p[tt % 2]
                    kb.y(xt, xkv[tc * 512 + tt * 128: tc * 512 + (tt + 1) * 128, :], tt % 2)
                    kb.v(lambda e, a=xt.ap, o=st1.ap: e.reduce_sum(o, a, axis=AX),
                         reads=[xt], writes=[st1])
                    kb.s(lambda e, a=xt.ap, o=sq.ap, ac=stss.ap:
                         e.activation(o, a, AF.Square, accum_out=ac),
                         reads=[xt], writes=[sq, stss])
                    kb.v(lambda e, a=st1.ap, o=stm.ap:
                         e.tensor_scalar(o, a, 1.0 / D, None, ALU.mult),
                         reads=[st1], writes=[stm])
                    kb.v(lambda e, a=stm.ap, o=stm2.ap: e.tensor_tensor(o, a, a, ALU.mult),
                         reads=[stm], writes=[stm2])
                    kb.v(lambda e, a=stss.ap, m2=stm2.ap, o=stvar.ap:
                         e.scalar_tensor_tensor(o, a, 1.0 / D, m2, ALU.mult, ALU.subtract),
                         reads=[stss, stm2], writes=[stvar])
                    kb.s(lambda e, a=stvar.ap, o=stsd.ap, eb=epsb.ap:
                         e.activation(o, a, AF.Sqrt, bias=eb),
                         reads=[stvar, epsb], writes=[stsd])
                    kb.v(lambda e, a=stsd.ap, o=str_.ap: e.reciprocal(o, a),
                         reads=[stsd], writes=[str_])
                    kb.v(lambda e, m=stm.ap, r=str_.ap, o=stnb.ap:
                         e.tensor_scalar(o, m, r, -1.0, ALU.mult, ALU.mult),
                         reads=[stm, str_], writes=[stnb])
                    kb.v(lambda e, a=xt.ap, r=str_.ap, b=stnb.ap, o=xnf.ap:
                         e.tensor_scalar(o, a, r, b, ALU.mult, ALU.add),
                         reads=[xt, str_, stnb], writes=[xnf])
                    for rc in range(NRC):
                        p = pb[rc % 2]
                        kb.t(lambda e, o=p.ap[:, 0:128],
                             i=xnf.ap[:, rc * 128:(rc + 1) * 128], ident=idf.ap:
                             e.transpose(o, i, ident),
                             reads=[xnf, idf], writes=[p])
                        kb.s(lambda e, o=xnT_h[rc].ap[:, tt * 128:(tt + 1) * 128],
                             i=p.ap[:, 0:128]: e.copy(o, i),
                             reads=[p], wpart=[xnT_h[rc]])
                        kb.v(lambda e, o=xnT_l[rc].ap[:, tt * 128:(tt + 1) * 128],
                             i=p.ap[:, 0:128], hh=xnT_h[rc].ap[:, tt * 128:(tt + 1) * 128]:
                             e.scalar_tensor_tensor(o, i, 1.0, hh, ALU.mult, ALU.subtract),
                             reads=[p, xnT_h[rc]], wpart=[xnT_l[rc]])

                for rc in range(NRC):
                    wb, wh, wl = wbuf_p[rc % 2], wh_p[rc % 2], wl_p[rc % 2]
                    kb.y(wb, wk[rc * 128:(rc + 1) * 128, :], 2 + rc % 2)
                    split_w(wb, wh, wl)
                    first = rc == 0
                    last = rc == NRC - 1
                    for kvh in range(HKV):
                        p = pb[kvh]
                        for j, xs in enumerate((xnT_h[rc], xnT_l[rc])):
                            kb.t(lambda e, o=p.ap, w=wh.ap[:, kvh * 128:(kvh + 1) * 128],
                                 r=xs.ap, s0=(first and j == 0):
                                 e.matmul(o, w, r, start=s0, stop=False),
                                 reads=(wh, xnT_h[rc], xnT_l[rc]),
                                 wdep=[p] if (first and j == 0) else (),
                                 inc=False)
                    for kvh in range(HKV):
                        p = pb[kvh]
                        kb.t(lambda e, o=p.ap, w=wl.ap[:, kvh * 128:(kvh + 1) * 128],
                             r=xnT_h[rc].ap, s1=last:
                             e.matmul(o, w, r, start=False, stop=s1),
                             reads=(wh, wl, xnT_h[rc], xnT_l[rc]),
                             writes=[p] if last else (),
                             inc=True)
                for kvh in range(HKV):
                    cs = slice(tc * 512, (tc + 1) * 512)
                    rope(pb[kvh], cs, rCh, rCl, rSh, rSl, kf, rot, t1, t2, t3, kh, kl)
                    kb.g(kscr_h[kvh * 128:(kvh + 1) * 128, cs], kh, 0)
                    kb.g(kscr_l[kvh * 128:(kvh + 1) * 128, cs], kl, 1)

                for rc in range(NRC):
                    wb, wh, wl = wbuf_p[rc % 2], wh_p[rc % 2], wl_p[rc % 2]
                    kb.y(wb, wv[rc * 128:(rc + 1) * 128, :], 2 + rc % 2)
                    split_w(wb, wh, wl)
                    first = rc == 0
                    last = rc == NRC - 1
                    for kt in range(4):
                        for wt, dvc in ((wh, 0), (wh, 1), (wl, 0), (wl, 1)):
                            p = pb[kt * 2 + dvc]
                            st0 = first and wt is wh
                            kb.t(lambda e, o=p.ap, w=xnT_h[rc].ap[:, kt * 128:(kt + 1) * 128],
                                 r=wt.ap[:, dvc * 512:(dvc + 1) * 512], s0=st0:
                                 e.matmul(o, w, r, start=s0, stop=False),
                                 reads=(wh, wl, xnT_h[rc]),
                                 wdep=[p] if st0 else (),
                                 inc=False)
                    for kt in range(4):
                        for dvc in range(2):
                            p = pb[kt * 2 + dvc]
                            kb.t(lambda e, o=p.ap, w=xnT_l[rc].ap[:, kt * 128:(kt + 1) * 128],
                                 r=wh.ap[:, dvc * 512:(dvc + 1) * 512], s1=last:
                                 e.matmul(o, w, r, start=False, stop=s1),
                                 reads=(wh, wl, xnT_h[rc], xnT_l[rc]),
                                 writes=[p] if last else (),
                                 inc=True)
                for kt in range(4):
                    slot = tc * 4 + kt
                    for dvc in range(2):
                        p = pb[kt * 2 + dvc]
                        split_w(p, vh, vl)
                        ds = slice(dvc * 512, (dvc + 1) * 512)
                        kb.g(vscr_h[slot * 128:(slot + 1) * 128, ds], vh, 2)
                        kb.g(vscr_l[slot * 128:(slot + 1) * 128, ds], vl, 3)

                if tc >= 2:
                    ts_ = slice((tc - 2) * 512, (tc - 1) * 512)
                    for rc in range(NRC):
                        kb.g(xnscr_h[rc * 128:(rc + 1) * 128, ts_], xnT_h[rc], 4)
                        kb.g(xnscr_l[rc * 128:(rc + 1) * 128, ts_], xnT_l[rc], 5)
                    for rc in range(NRC):
                        _merge(xnT_h[rc].rds, ("g4", kb.cnt["g4"]))
                        _merge(xnT_l[rc].rds, ("g5", kb.cnt["g5"]))
            kb.barrier()
            kb.flush()
        if upto <= 1:
            return nc

        with ExitStack() as ps:
            sb = lambda n, shp, dt: ps.enter_context(nc.sbuf_tensor(n, shp, dt))[:]
            rCh = Buf(sb("rCh2", [128, TOWN], F32))
            rCl = Buf(sb("rCl2", [128, TOWN], BF16))
            rSh = Buf(sb("rSh2", [128, TOWN], F32))
            rSl = Buf(sb("rSl2", [128, TOWN], BF16))
            kb.y(rCh, ropeCh[:, 1024:2048], 4)
            kb.y(rCl, ropeCl[:, 1024:2048], 4)
            kb.y(rSh, ropeSh[:, 1024:2048], 4)
            kb.y(rSl, ropeSl[:, 1024:2048], 4)
            xnT_h = bufs_of(sb("xnT_h2", [128, NRC * 1024], BF16), NRC, 1024)
            xnT_l = bufs_of(sb("xnT_l2", [128, NRC * 1024], BF16), NRC, 1024)
            for rc in range(NRC):
                kb.y(xnT_h[rc], xnscr_h[rc * 128:(rc + 1) * 128, :], 4)
                kb.y(xnT_l[rc], xnscr_l[rc * 128:(rc + 1) * 128, :], 4)
            ld_all = ("y4", kb.cnt["y4"])
            for b in [rCh, rCl, rSh, rSl] + xnT_h + xnT_l:
                b.wrs = dict([ld_all])
            wbuf_p = [Buf(sb(f"wb2{i}", [128, 512], F32)) for i in range(2)]
            wh_p = [Buf(sb(f"wh2{i}", [128, 512], BF16)) for i in range(2)]
            wl_p = [Buf(sb(f"wl2{i}", [128, 512], BF16)) for i in range(2)]
            kf = Buf(sb("kf2", [128, 512], F32))
            rot = Buf(sb("rot2", [128, 512], F32))
            t1 = Buf(sb("t1_2", [128, 512], F32))
            t2 = Buf(sb("t2_2", [128, 512], F32))
            t3 = Buf(sb("t3_2", [128, 512], F32))
            kh = Buf(sb("kh2", [128, 512], BF16))
            kl = Buf(sb("kl2", [128, 512], BF16))
            pb = [Buf(ps.enter_context(nc.psum_tensor(f"pb2{i}", [128, 512], F32))[:])
                  for i in range(8)]

            for qhg in range(8):
                for rc in range(NRC):
                    wb, wh, wl = wbuf_p[rc % 2], wh_p[rc % 2], wl_p[rc % 2]
                    kb.y(wb, wq[rc * 128:(rc + 1) * 128, qhg * 512:(qhg + 1) * 512], 2 + rc % 2)
                    split_w(wb, wh, wl)
                    first = rc == 0
                    last = rc == NRC - 1
                    for hi in range(4):
                        for xs, tcq in ((xnT_h[rc], 0), (xnT_h[rc], 1),
                                        (xnT_l[rc], 0), (xnT_l[rc], 1)):
                            p = pb[hi * 2 + tcq]
                            st0 = first and xs is xnT_h[rc]
                            kb.t(lambda e, o=p.ap, w=wh.ap[:, hi * 128:(hi + 1) * 128],
                                 r=xs.ap[:, tcq * 512:(tcq + 1) * 512], s0=st0:
                                 e.matmul(o, w, r, start=s0, stop=False),
                                 reads=(wh, xnT_h[rc], xnT_l[rc]),
                                 wdep=[p] if st0 else (),
                                 inc=False)
                    for hi in range(4):
                        for tcq in range(2):
                            p = pb[hi * 2 + tcq]
                            kb.t(lambda e, o=p.ap, w=wl.ap[:, hi * 128:(hi + 1) * 128],
                                 r=xnT_h[rc].ap[:, tcq * 512:(tcq + 1) * 512], s1=last:
                                 e.matmul(o, w, r, start=False, stop=s1),
                                 reads=(wh, wl, xnT_h[rc], xnT_l[rc]),
                                 writes=[p] if last else (),
                                 inc=True)
                for hi in range(4):
                    qh = qhg * 4 + hi
                    for tcq in range(2):
                        p = pb[hi * 2 + tcq]
                        cs = slice(tcq * 512, (tcq + 1) * 512)
                        rope(p, cs, rCh, rCl, rSh, rSl, kf, rot, t1, t2, t3, kh, kl)
                        kb.g(qscr_h[qh * 128:(qh + 1) * 128, cs], kh, 0)
                        kb.g(qscr_l[qh * 128:(qh + 1) * 128, cs], kl, 1)
            kb.barrier()
            kb.flush()
        if upto <= 2:
            return nc

        with ExitStack() as ps:
            sb = lambda n, shp, dt: ps.enter_context(nc.sbuf_tensor(n, shp, dt))[:]
            mks = Buf(sb("mks", [128, 8 * NW * 128], F32))
            for l in range(8):
                kb.y(mks, masks[l * 128:(l + 1) * 128, :], 4, wpart=True,
                     dst_ap=mks.ap[:, l * 1152:(l + 1) * 1152])
            qb_h = [Buf(sb(f"qbh{i}", [128, 512], BF16)) for i in range(2)]
            qb_l = [Buf(sb(f"qbl{i}", [128, 512], BF16)) for i in range(2)]
            kB_h = [Buf(sb(f"kbh{i}", [128, 1536], BF16)) for i in range(2)]
            kB_l = [Buf(sb(f"kbl{i}", [128, 1536], BF16)) for i in range(2)]
            vB_h = [Buf(sb(f"vbh{i}", [128, 1536], BF16)) for i in range(2)]
            vB_l = [Buf(sb(f"vbl{i}", [128, 1536], BF16)) for i in range(2)]
            AT_hp = [Buf(sb(f"ATh{i}", [128, 12 * 512], BF16)) for i in range(2)]
            AT_lp = [Buf(sb(f"ATl{i}", [128, 12 * 512], BF16)) for i in range(2)]
            Af_p = [Buf(sb(f"Af{i}", [128, 1152], F32)) for i in range(2)]
            Ahh_p = [Buf(sb(f"Ahh{i}", [128, 1152], BF16)) for i in range(2)]
            All_p = [Buf(sb(f"All{i}", [128, 1152], BF16)) for i in range(2)]
            Sm_p = [Buf(sb(f"Sm{i}", [128, 1152], F32)) for i in range(2)]
            stm_p = [Buf(sb(f"stm3{i}", [128, 1], F32)) for i in range(2)]
            stnb_p = [Buf(sb(f"stnb3{i}", [128, 1], F32)) for i in range(2)]
            stl_p = [Buf(sb(f"stl3{i}", [128, 1], F32)) for i in range(2)]
            str_p = [Buf(sb(f"str3{i}", [128, 1], F32)) for i in range(2)]
            oh_p = [Buf(sb(f"oh3{i}", [128, 512], BF16)) for i in range(2)]
            ol_p = [Buf(sb(f"ol3{i}", [128, 512], BF16)) for i in range(2)]
            pscT = ps.enter_context(nc.psum_tensor("pscT", [128, 1536], F32))[:]
            psc = [Buf(pscT[:, i * 512:(i + 1) * 512]) for i in range(3)]
            ptr = [Buf(ps.enter_context(nc.psum_tensor(f"ptr{i}", [128, 1024], BF16))[:])
                   for i in range(2)]
            pav = [Buf(ps.enter_context(nc.psum_tensor(f"pav{i}", [128, 512], F32))[:])
                   for i in range(2)]

            for par in range(2):
                for s_ in range(12):
                    for l4 in range(4):
                        if l4 <= s_ <= l4 + 8:
                            continue
                        for ab in (AT_hp[par], AT_lp[par]):
                            kb.gc(lambda e,
                                  o=ab.ap[:, s_ * 512 + l4 * 128: s_ * 512 + (l4 + 1) * 128]:
                                  e.memset(o, 0.0), wpart=[ab])

            for g2 in range(2):
                for kvh in range(HKV):
                    kbh, kbl = kB_h[kvh % 2], kB_l[kvh % 2]
                    vbh, vbl = vB_h[kvh % 2], vB_l[kvh % 2]
                    ks_ = slice(g2 * 512, g2 * 512 + 1536)
                    kb.y(kbh, kscr_h[kvh * 128:(kvh + 1) * 128, ks_], 0 + kvh % 2)
                    kb.y(kbl, kscr_l[kvh * 128:(kvh + 1) * 128, ks_], 2 + kvh % 2)
                    voff = (g2 * 4) * 128 * 1024 + kvh * 128
                    vap = [[1024, 128], [128 * 1024, 12], [1, 128]]
                    kb.y(vbh, bass.AP(vscr_h.tensor, voff, vap), 5 + kvh % 2)
                    kb.y(vbl, bass.AP(vscr_l.tensor, voff, vap), 7 + kvh % 2)
                    for hi in range(4):
                        qh = kvh * 4 + hi
                        qbh, qbl = qb_h[hi % 2], qb_l[hi % 2]
                        qs_ = slice(g2 * 512, (g2 + 1) * 512)
                        kb.y(qbh, qscr_h[qh * 128:(qh + 1) * 128, qs_], 9 + hi % 2)
                        kb.y(qbl, qscr_l[qh * 128:(qh + 1) * 128, qs_], 11 + hi % 2)
                        pav_b = pav[qh % 2]
                        ATh = AT_hp[qh % 2]
                        ATl = AT_lp[qh % 2]

                        def scores_mm(l4, qbh=qbh, qbl=qbl, kbh=kbh, kbl=kbl):
                            chunks = [(0, 512), (512, 512), (1024, 128)]
                            for qq, kk2, st0, sp1 in ((qbh, kbh, True, False),
                                                      (qbh, kbl, False, False),
                                                      (qbl, kbh, False, True)):
                                for ci, (c0, cw) in enumerate(chunks):
                                    p = psc[ci]
                                    kb.t(lambda e, o=p.ap[:, 0:cw],
                                         w=qq.ap[:, l4 * 128:(l4 + 1) * 128],
                                         r=kk2.ap[:, l4 * 128 + c0: l4 * 128 + c0 + cw],
                                         s0=st0, s1=sp1:
                                         e.matmul(o, w, r, start=s0, stop=s1),
                                         reads=(qbh, qbl, kbh, kbl),
                                         writes=[p] if sp1 else (),
                                         wdep=[p] if st0 else (),
                                         inc=sp1)

                        def softmax_post(l4, g2=g2):
                            l = g2 * 4 + l4
                            stm = stm_p[l4 % 2]
                            stnb = stnb_p[l4 % 2]
                            stl = stl_p[l4 % 2]
                            Af = Af_p[l4 % 2]
                            Sm = Sm_p[l4 % 2]
                            kb.v(lambda e, o=Sm.ap, a=pscT[:, 0:1152],
                                 m=mks.ap[:, l * 1152:(l + 1) * 1152]:
                                 e.tensor_tensor(o, a, m, ALU.add),
                                 reads=[mks, psc[0], psc[1], psc[2]],
                                 writes=[Sm])
                            kb.v(lambda e, o=stm.ap, i=Sm.ap: e.reduce_max(o, i, axis=AX),
                                 reads=[Sm], writes=[stm])
                            kb.v(lambda e, o=stnb.ap, i=stm.ap:
                                 e.tensor_scalar(o, i, -SM, None, ALU.mult),
                                 reads=[stm], writes=[stnb])
                            kb.s(lambda e, o=Af.ap, i=Sm.ap, b=stnb.ap, ls=stl.ap:
                                 e.activation(o, i, AF.Exp, bias=b, scale=SM,
                                              accum_out=ls),
                                 reads=[Sm, stnb],
                                 writes=[Af, stl])

                        def split_transpose(l4, ATh=ATh, ATl=ATl):
                            stl = stl_p[l4 % 2]
                            str_ = str_p[l4 % 2]
                            Af = Af_p[l4 % 2]
                            Ahh = Ahh_p[l4 % 2]
                            All = All_p[l4 % 2]
                            kb.v(lambda e, o=str_.ap, i=stl.ap: e.reciprocal(o, i),
                                 reads=[stl], writes=[str_])
                            kb.v(lambda e, o=Ahh.ap, i=Af.ap, r=str_.ap:
                                 e.tensor_scalar(o, i, r, None, ALU.mult),
                                 reads=[Af, str_], writes=[Ahh])
                            kb.v(lambda e, o=All.ap, i=Af.ap, r=str_.ap, hh=Ahh.ap:
                                 e.scalar_tensor_tensor(o, i, r, hh, ALU.mult, ALU.subtract),
                                 reads=[Af, str_, Ahh], writes=[All])
                            for w in range(NW):
                                s_ = l4 + w
                                for j, (src, dst) in enumerate(((Ahh, ATh), (All, ATl))):
                                    p = ptr[(2 * w + j) % 2]
                                    kb.t(lambda e, o=p.ap[:, 0:128],
                                         i=src.ap[:, w * 128:(w + 1) * 128], ident=idb.ap:
                                         e.transpose(o, i, ident),
                                         reads=[src, idb], writes=[p])
                                    dsl_ap = dst.ap[:, s_ * 512 + l4 * 128:
                                                    s_ * 512 + (l4 + 1) * 128]
                                    if j == 0:
                                        kb.s(lambda e, o=dsl_ap, i=p.ap[:, 0:128]:
                                             e.copy(o, i), reads=[p], wpart=[dst])
                                    else:
                                        kb.v(lambda e, o=dsl_ap, i=p.ap[:, 0:128]:
                                             e.tensor_copy(o, i), reads=[p], wpart=[dst])

                        for step in range(5):
                            if step < 4:
                                scores_mm(step)
                            if step >= 1:
                                split_transpose(step - 1)
                            if step < 4:
                                softmax_post(step)
                        for s_ in range(12):
                            for aa in (ATh, ATl):
                                first = (s_ == 0 and aa is ATh)
                                kb.t(lambda e, o=pav_b.ap, w=vbh.ap[:, s_ * 128:(s_ + 1) * 128],
                                     r=aa.ap[:, s_ * 512:(s_ + 1) * 512], s0=first:
                                     e.matmul(o, w, r, start=s0, stop=False),
                                     reads=(vbh, ATh, ATl),
                                     wdep=[pav_b] if first else (),
                                     inc=False)
                        for s_ in range(12):
                            last = s_ == 11
                            kb.t(lambda e, o=pav_b.ap, w=vbl.ap[:, s_ * 128:(s_ + 1) * 128],
                                 r=ATh.ap[:, s_ * 512:(s_ + 1) * 512], s1=last:
                                 e.matmul(o, w, r, start=False, stop=s1),
                                 reads=(vbh, vbl, ATh, ATl),
                                 writes=[pav_b] if last else (),
                                 inc=last)
                        oh = oh_p[qh % 2]
                        ol = ol_p[qh % 2]
                        split_w(pav_b, oh, ol)
                        qs2 = slice(g2 * 512, (g2 + 1) * 512)
                        kb.g(oscr_h[qh * 128:(qh + 1) * 128, qs2], oh, 0 + qh % 2)
                        kb.g(oscr_l[qh * 128:(qh + 1) * 128, qs2], ol, 2 + qh % 2)
            kb.barrier()
            kb.flush()
        if upto <= 3:
            return nc

        with ExitStack() as ps:
            sb = lambda n, shp, dt: ps.enter_context(nc.sbuf_tensor(n, shp, dt))[:]
            oT_h = bufs_of(sb("oTh", [128, NRC * 512], BF16), NRC, 512)
            oT_l = bufs_of(sb("oTl", [128, NRC * 512], BF16), NRC, 512)
            wbuf_p = [Buf(sb(f"wb3{i}", [128, 1024], F32)) for i in range(2)]
            wh_p = [Buf(sb(f"wh3{i}", [128, 1024], BF16)) for i in range(2)]
            wl_p = [Buf(sb(f"wl3{i}", [128, 1024], BF16)) for i in range(2)]
            sf = Buf(sb("sf3", [128, 512], F32))
            pb = [Buf(ps.enter_context(nc.psum_tensor(f"pb3{i}", [128, 512], F32))[:])
                  for i in range(8)]
            for tcq in range(2):
                ts_ = slice(tcq * 512, (tcq + 1) * 512)
                for oc in range(NRC):
                    kb.y(oT_h[oc], oscr_h[oc * 128:(oc + 1) * 128, ts_], 4)
                    kb.y(oT_l[oc], oscr_l[oc * 128:(oc + 1) * 128, ts_], 4)
                ld_all = ("y4", kb.cnt["y4"])
                for b in oT_h + oT_l:
                    b.wrs = dict([ld_all])
                for rnd in range(4):
                    for oc in range(NRC):
                        wb, wh, wl = wbuf_p[oc % 2], wh_p[oc % 2], wl_p[oc % 2]
                        kb.y(wb, wo[oc * 128:(oc + 1) * 128, rnd * 1024:(rnd + 1) * 1024],
                             2 + oc % 2)
                        split_w(wb, wh, wl)
                        first = oc == 0
                        last = oc == NRC - 1
                        for dch in range(8):
                            p = pb[dch]
                            for j, xs in enumerate((oT_h[oc], oT_l[oc])):
                                kb.t(lambda e, o=p.ap, w=wh.ap[:, dch * 128:(dch + 1) * 128],
                                     r=xs.ap, s0=(first and j == 0):
                                     e.matmul(o, w, r, start=s0, stop=False),
                                     reads=(wh, oT_h[oc], oT_l[oc]),
                                     wdep=[p] if (first and j == 0) else (),
                                     inc=False)
                        for dch in range(8):
                            p = pb[dch]
                            kb.t(lambda e, o=p.ap, w=wl.ap[:, dch * 128:(dch + 1) * 128],
                                 r=oT_h[oc].ap, s1=last:
                                 e.matmul(o, w, r, start=False, stop=s1),
                                 reads=(wh, wl, oT_h[oc], oT_l[oc]),
                                 writes=[p] if last else (),
                                 inc=True)
                    for dch in range(8):
                        p = pb[dch]
                        dp = rnd * 8 + dch
                        kb.v(lambda e, o=sf.ap, i=p.ap: e.tensor_copy(o, i),
                             reads=[p], writes=[sf])
                        kb.g(sscr[dp * 128:(dp + 1) * 128, ts_], sf, 0)
            kb.barrier()
            kb.flush()
        if upto <= 4:
            return nc

        with ExitStack() as ps:
            sb = lambda n, shp, dt: ps.enter_context(nc.sbuf_tensor(n, shp, dt))[:]
            for nm, (zt, st_, oh_, ol_, n_, w_) in enumerate((
                    (zg, sg, nzsg_h, nzsg_l, NRC, FF),
                    (zu, su, nzsu_h, nzsu_l, NRC, FF),
                    (zd, sd, nzsd_h, nzsd_l, NCC, D))):
                half = w_ // 2
                zb = Buf(sb(f"zb{nm}", [n_, half], U8))
                sbf = Buf(sb(f"sb{nm}", [n_, half], F32))
                tb = Buf(sb(f"tb{nm}", [n_, half], F32))
                hb = Buf(sb(f"hb{nm}", [n_, half], BF16))
                lb = Buf(sb(f"lb{nm}", [n_, half], BF16))
                for hf in range(2):
                    cs = slice(hf * half, (hf + 1) * half)
                    kb.y(zb, zt[:, cs], nm * 2)
                    kb.y(sbf, st_[:, cs], nm * 2 + 1)
                    kb.v(lambda e, o=tb.ap, a=zb.ap, b=sbf.ap:
                         e.tensor_tensor(o, a, b, ALU.mult),
                         reads=[zb, sbf], writes=[tb])
                    kb.s(lambda e, o=hb.ap, i=tb.ap:
                         e.activation(o, i, AF.Copy, scale=-1.0),
                         reads=[tb], writes=[hb])
                    kb.v(lambda e, o=lb.ap, i=tb.ap, hh=hb.ap:
                         e.scalar_tensor_tensor(o, i, -1.0, hh, ALU.mult, ALU.subtract),
                         reads=[tb, hb], writes=[lb])
                    kb.g(oh_[:, cs], hb, nm * 2)
                    kb.g(ol_[:, cs], lb, nm * 2 + 1)
            kb.barrier()
            kb.flush()

        with ExitStack() as ps:
            sb = lambda n, shp, dt: ps.enter_context(nc.sbuf_tensor(n, shp, dt))[:]
            xnT_h = bufs_of(sb("xnT_h4", [128, NRC * 1024], BF16), NRC, 1024)
            xnT_l = bufs_of(sb("xnT_l4", [128, NRC * 1024], BF16), NRC, 1024)
            for rc in range(NRC):
                kb.y(xnT_h[rc], xnscr_h[rc * 128:(rc + 1) * 128, :], 4)
                kb.y(xnT_l[rc], xnscr_l[rc * 128:(rc + 1) * 128, :], 4)
            gs32 = Buf(sb("gs32", [128, 1024], BF16))
            kb.y(gs32, gsel32[:, :], 4)
            ld_all = ("y4", kb.cnt["y4"])
            for b in xnT_h + xnT_l + [gs32]:
                b.wrs = dict([ld_all])
            Rf = Buf(sb("Rf", [32, 1024], F32))
            Rbh = Buf(sb("Rbh", [32, 1024], BF16))
            Rbl = Buf(sb("Rbl", [32, 1024], BF16))
            with ExitStack() as prs:
                pR = Buf(prs.enter_context(nc.psum_tensor("pR", [32, 512], F32))[:])
                for tcq in range(2):
                    n = 0
                    for rc in range(NRC):
                        for xs in (xnT_h[rc], xnT_l[rc]):
                            first = n == 0
                            last = n == 2 * NRC - 1
                            kb.t(lambda e, o=pR.ap, w=gs32.ap[:, rc * 32:(rc + 1) * 32],
                                 r=xs.ap[:, tcq * 512:(tcq + 1) * 512], s0=first, s1=last:
                                 e.matmul(o, w, r, start=s0, stop=s1),
                                 reads=(gs32, xnT_h[rc], xnT_l[rc]),
                                 writes=[pR] if last else (),
                                 wdep=[pR] if first else (),
                                 inc=last)
                            n += 1
                    kb.v(lambda e, o=Rf.ap[:, tcq * 512:(tcq + 1) * 512], i=pR.ap:
                         e.tensor_copy(o, i), reads=[pR], wpart=[Rf])
            split_w(Rf, Rbh, Rbl)

            qw_p = [Buf(sb(f"qw4{i}", [128, 256], BF16)) for i in range(4)]
            sr_p = [Buf(sb(f"sr4{i}", [128, 256], F32)) for i in range(4)]
            wpf_p = [Buf(sb(f"wpf{i}", [128, 256], F32)) for i in range(4)]
            wph_p = [Buf(sb(f"wph{i}", [128, 256], BF16)) for i in range(4)]
            wpl_p = [Buf(sb(f"wpl{i}", [128, 256], BF16)) for i in range(4)]
            nz_h = [Buf(sb(f"nzh{i}", [32, 256], BF16)) for i in range(2)]
            nz_l = [Buf(sb(f"nzl{i}", [32, 256], BF16)) for i in range(2)]
            gsl = Buf(sb("gsl", [128, 512], F32))
            uf = Buf(sb("uf", [128, 512], F32))
            hf = Buf(sb("hf", [128, 512], F32))
            hh = Buf(sb("hh", [128, 512], BF16))
            hl = Buf(sb("hl", [128, 512], BF16))
            pb = [Buf(ps.enter_context(nc.psum_tensor(f"pb4{i}", [128, 512], F32))[:])
                  for i in range(8)]
            nzscr = ((nzsg_h, nzsg_l), (nzsu_h, nzsu_l))

            for cb in range(43):
                cs = slice(cb * 256, (cb + 1) * 256)
                for mi in range(2):
                    kb.y(nz_h[mi], nzscr[mi][0][:, cs], 5 + mi)
                    kb.y(nz_l[mi], nzscr[mi][1][:, cs], 7 + mi)
                for mi in range(2):
                    for cc in range(2):
                        for tcq in range(2):
                            p = pb[mi * 4 + cc * 2 + tcq]
                            ops = [(nz_h[mi], Rbh, True), (nz_h[mi], Rbl, False),
                                   (nz_l[mi], Rbh, False)]
                            for i, (zz, rr, st0) in enumerate(ops):
                                kb.t(lambda e, o=p.ap, w=zz.ap[:, cc * 128:(cc + 1) * 128],
                                     r=rr.ap[:, tcq * 512:(tcq + 1) * 512], s0=st0:
                                     e.matmul(o, w, r, start=s0, stop=False),
                                     reads=(nz_h[mi], nz_l[mi], Rbh, Rbl),
                                     wdep=[p] if i == 0 else (),
                                     inc=(i == 2))
                for rc in range(NRC):
                    for mi, (qwt, sst) in enumerate(((qwg, sg), (qwu, su))):
                        pi = (rc % 2) * 2 + mi
                        qwb, srb = qw_p[pi], sr_p[pi]
                        wpf, wph, wpl = wpf_p[pi], wph_p[pi], wpl_p[pi]
                        kb.g(qwb, qwt[rc * 128:(rc + 1) * 128, cs], 2 + pi)
                        kb.g(srb, bass.AP(sst.tensor, rc * FF + cb * 256, [[0, 128], [1, 256]]),
                             8 + pi)
                        kb.v(lambda e, o=wpf.ap, a=qwb.ap, b=srb.ap:
                             e.tensor_tensor(o, a, b, ALU.mult),
                             reads=[qwb, srb], writes=[wpf])
                        split_w(wpf, wph, wpl)
                        last = rc == NRC - 1
                        for cc in range(2):
                            for xs, tcq in ((xnT_h[rc], 0), (xnT_h[rc], 1),
                                            (xnT_l[rc], 0), (xnT_l[rc], 1)):
                                p = pb[mi * 4 + cc * 2 + tcq]
                                kb.t(lambda e, o=p.ap,
                                     w=wph.ap[:, cc * 128:(cc + 1) * 128],
                                     r=xs.ap[:, tcq * 512:(tcq + 1) * 512]:
                                     e.matmul(o, w, r, start=False, stop=False),
                                     reads=(wph, xnT_h[rc], xnT_l[rc]),
                                     inc=False)
                        for cc in range(2):
                            for tcq in range(2):
                                p = pb[mi * 4 + cc * 2 + tcq]
                                kb.t(lambda e, o=p.ap,
                                     w=wpl.ap[:, cc * 128:(cc + 1) * 128],
                                     r=xnT_h[rc].ap[:, tcq * 512:(tcq + 1) * 512], s1=last:
                                     e.matmul(o, w, r, start=False, stop=s1),
                                     reads=(wph, wpl, xnT_h[rc], xnT_l[rc]),
                                     writes=[p] if last else (),
                                     inc=True)
                for cc in range(2):
                    ccg = cb * 2 + cc
                    for tcq in range(2):
                        pg = pb[0 * 4 + cc * 2 + tcq]
                        pu = pb[1 * 4 + cc * 2 + tcq]
                        kb.s(lambda e, o=gsl.ap, i=pg.ap: e.activation(o, i, AF.Silu),
                             reads=[pg], writes=[gsl])
                        kb.v(lambda e, o=uf.ap, i=pu.ap: e.tensor_copy(o, i),
                             reads=[pu], writes=[uf])
                        kb.v(lambda e, o=hf.ap, a=gsl.ap, b=uf.ap:
                             e.tensor_tensor(o, a, b, ALU.mult),
                             reads=[gsl, uf], writes=[hf])
                        split_w(hf, hh, hl)
                        ts_ = slice(tcq * 512, (tcq + 1) * 512)
                        kb.g(hscr_h[ccg * 128:(ccg + 1) * 128, ts_], hh, 0)
                        kb.g(hscr_l[ccg * 128:(ccg + 1) * 128, ts_], hl, 1)
            kb.barrier()
            kb.flush()

        with ExitStack() as ps:
            sb = lambda n, shp, dt: ps.enter_context(nc.sbuf_tensor(n, shp, dt))[:]
            gs86g = Buf(sb("gs86g", [128, 86 * 86], BF16))
            kb.y(gs86g, gsel86[:, :], 4)
            h_p = [Buf(sb(f"hp{i}", [128, 512], BF16)) for i in range(4)]
            Rhf_g = Buf(sb("Rhfg", [86, 1024], F32))
            pR5 = Buf(ps.enter_context(nc.psum_tensor("pR5", [86, 512], F32))[:])
            for tcq in range(2):
                ts_ = slice(tcq * 512, (tcq + 1) * 512)
                n = 0
                for cc in range(NCC):
                    hb_ = h_p[cc % 2]
                    lb_ = h_p[2 + cc % 2]
                    kb.y(hb_, hscr_h[cc * 128:(cc + 1) * 128, ts_], cc % 2)
                    kb.y(lb_, hscr_l[cc * 128:(cc + 1) * 128, ts_], 2 + cc % 2)
                    for xs in (hb_, lb_):
                        first = n == 0
                        last = n == 2 * NCC - 1
                        kb.t(lambda e, o=pR5.ap, w=gs86g.ap[:, cc * 86:(cc + 1) * 86],
                             r=xs.ap, s0=first, s1=last:
                             e.matmul(o, w, r, start=s0, stop=s1),
                             reads=(gs86g, hb_, lb_),
                             writes=[pR5] if last else (),
                             wdep=[pR5] if first else (),
                             inc=(n % 2 == 1))
                        n += 1
                kb.v(lambda e, o=Rhf_g.ap[:, ts_], i=pR5.ap: e.tensor_copy(o, i),
                     reads=[pR5], wpart=[Rhf_g])
            Rhh_g = Buf(sb("Rhhg", [86, 1024], BF16))
            Rhl_g = Buf(sb("Rhlg", [86, 1024], BF16))
            split_w(Rhf_g, Rhh_g, Rhl_g)
            rh_scr_h = SCR("rh_scr_h", [86, 1024], BF16)
            rh_scr_l = SCR("rh_scr_l", [86, 1024], BF16)
            kb.g(rh_scr_h[:, :], Rhh_g, 0)
            kb.g(rh_scr_l[:, :], Rhl_g, 1)
            kb.barrier()
            kb.flush()

        with ExitStack() as ps:
            sb = lambda n, shp, dt: ps.enter_context(nc.sbuf_tensor(n, shp, dt))[:]
            Rhh = Buf(sb("Rhh", [86, 1024], BF16))
            Rhl = Buf(sb("Rhl", [86, 1024], BF16))
            kb.y(Rhh, rh_scr_h[:, :], 4)
            kb.y(Rhl, rh_scr_l[:, :], 4)
            ld_all = ("y4", kb.cnt["y4"])
            Rhh.wrs = dict([ld_all])
            Rhl.wrs = dict([ld_all])
            qw_p = [Buf(sb(f"qw5{i}", [128, 512], BF16)) for i in range(2)]
            sr_p = [Buf(sb(f"sr5{i}", [128, 512], F32)) for i in range(2)]
            wpf_p = [Buf(sb(f"wpf5{i}", [128, 512], F32)) for i in range(2)]
            wph_p = [Buf(sb(f"wph5{i}", [128, 512], BF16)) for i in range(2)]
            wpl_p = [Buf(sb(f"wpl5{i}", [128, 512], BF16)) for i in range(2)]
            hh_p = [Buf(sb(f"hh5{i}", [128, 1024], BF16)) for i in range(2)]
            hl_p = [Buf(sb(f"hl5{i}", [128, 1024], BF16)) for i in range(2)]
            nzd_h = Buf(sb("nzdh", [86, 512], BF16))
            nzd_l = Buf(sb("nzdl", [86, 512], BF16))
            gbuf_p = [Buf(sb(f"gbuf{i}", [128, 512], F32)) for i in range(2)]
            stb_p = [Buf(sb(f"stbuf{i}", [128, 512], F32)) for i in range(2)]
            sgb_p = [Buf(sb(f"sgbuf{i}", [128, 512], F32)) for i in range(2)]
            xb_p = [Buf(sb(f"xbuf{i}", [128, 128], F32)) for i in range(2)]
            ob_p = [Buf(sb(f"obuf{i}", [128, 128], F32)) for i in range(2)]
            pb = [Buf(ps.enter_context(nc.psum_tensor(f"pb5{i}", [128, 512], F32))[:])
                  for i in range(8)]

            for rnd in range(8):
                rs = slice(rnd * 512, (rnd + 1) * 512)
                kb.y(nzd_h, nzsd_h[:, rs], 5)
                kb.y(nzd_l, nzsd_l[:, rs], 6)
                for dch in range(4):
                    for tcq in range(2):
                        p = pb[dch * 2 + tcq]
                        ops = [(nzd_h, Rhh, True), (nzd_h, Rhl, False), (nzd_l, Rhh, False)]
                        for i, (zz, rr, st0) in enumerate(ops):
                            kb.t(lambda e, o=p.ap, w=zz.ap[:, dch * 128:(dch + 1) * 128],
                                 r=rr.ap[:, tcq * 512:(tcq + 1) * 512], s0=st0:
                                 e.matmul(o, w, r, start=s0, stop=False),
                                 reads=(nzd_h, nzd_l, Rhh, Rhl),
                                 wdep=[p] if i == 0 else (),
                                 inc=(i == 2))
                for cc in range(NCC):
                    qwb, srb = qw_p[cc % 2], sr_p[cc % 2]
                    wpf, wph, wpl = wpf_p[cc % 2], wph_p[cc % 2], wpl_p[cc % 2]
                    hhb, hlb = hh_p[cc % 2], hl_p[cc % 2]
                    kb.y(hhb, hscr_h[cc * 128:(cc + 1) * 128, :], 0 + cc % 2)
                    kb.y(hlb, hscr_l[cc * 128:(cc + 1) * 128, :], 2 + cc % 2)
                    kb.g(qwb, qwd[cc * 128:(cc + 1) * 128, rs], 2 + cc % 2)
                    kb.g(srb, bass.AP(sd.tensor, cc * D + rnd * 512, [[0, 128], [1, 512]]),
                         8 + cc % 2)
                    kb.v(lambda e, o=wpf.ap, a=qwb.ap, b=srb.ap:
                         e.tensor_tensor(o, a, b, ALU.mult),
                         reads=[qwb, srb], writes=[wpf])
                    split_w(wpf, wph, wpl)
                    last = cc == NCC - 1
                    for dch in range(4):
                        for xs, tcq in ((hhb, 0), (hhb, 1), (hlb, 0), (hlb, 1)):
                            p = pb[dch * 2 + tcq]
                            kb.t(lambda e, o=p.ap, w=wph.ap[:, dch * 128:(dch + 1) * 128],
                                 r=xs.ap[:, tcq * 512:(tcq + 1) * 512]:
                                 e.matmul(o, w, r, start=False, stop=False),
                                 reads=(wph, hhb, hlb),
                                 inc=False)
                    for dch in range(4):
                        for tcq in range(2):
                            p = pb[dch * 2 + tcq]
                            kb.t(lambda e, o=p.ap, w=wpl.ap[:, dch * 128:(dch + 1) * 128],
                                 r=hhb.ap[:, tcq * 512:(tcq + 1) * 512], s1=last:
                                 e.matmul(o, w, r, start=False, stop=s1),
                                 reads=(wph, wpl, hhb, hlb),
                                 writes=[p] if last else (),
                                 inc=True)
                for dch in range(4):
                    dp = rnd * 4 + dch
                    for tcq in range(2):
                        p = pb[dch * 2 + tcq]
                        ts_ = slice(tcq * 512, (tcq + 1) * 512)
                        par = (dch * 2 + tcq) % 2
                        gbuf, stbuf, sgbuf = gbuf_p[par], stb_p[par], sgb_p[par]
                        kb.s(lambda e, o=gbuf.ap, i=p.ap: e.activation(o, i, AF.Silu),
                             reads=[p], writes=[gbuf])
                        kb.y(stbuf, sscr[dp * 128:(dp + 1) * 128, ts_], 10 + par)
                        kb.v(lambda e, o=sgbuf.ap, a=gbuf.ap, b=stbuf.ap:
                             e.tensor_tensor(o, a, b, ALU.mult),
                             reads=[gbuf, stbuf], writes=[sgbuf])
                        for tt in range(4):
                            xpar = tt % 2
                            xbuf, obuf = xb_p[xpar], ob_p[xpar]
                            kb.t(lambda e, o=p.ap[:, 0:128],
                                 i=sgbuf.ap[:, tt * 128:(tt + 1) * 128], ident=idf.ap:
                                 e.transpose(o, i, ident),
                                 reads=[sgbuf, idf], writes=[p])
                            rows = slice(1024 + tcq * 512 + tt * 128,
                                         1024 + tcq * 512 + (tt + 1) * 128)
                            cols = slice(dp * 128, (dp + 1) * 128)
                            kb.y(xbuf, xkv[rows, cols], 12 + xpar)
                            kb.v(lambda e, o=obuf.ap, a=xbuf.ap, b=p.ap[:, 0:128]:
                                 e.tensor_tensor(o, a, b, ALU.add),
                                 reads=[xbuf, p], writes=[obuf])
                            orows = slice(tcq * 512 + tt * 128, tcq * 512 + (tt + 1) * 128)
                            kb.g(out[orows, cols], obuf, 4 + xpar)
            kb.barrier()
            kb.flush()

    return nc


def _host_inputs(L0_attn_x, L0_attn_wq, L0_attn_wk, L0_attn_wv, L0_attn_wo,
                 L0_ffn_qw_gate, L0_ffn_z_gate, L0_ffn_s_gate,
                 L0_ffn_qw_up, L0_ffn_z_up, L0_ffn_s_up,
                 L0_ffn_qw_down, L0_ffn_z_down, L0_ffn_s_down):
    x = np.asarray(L0_attn_x, dtype=np.float32)
    shared = {
        "wq": np.ascontiguousarray(np.asarray(L0_attn_wq, np.float32)),
        "wk": np.ascontiguousarray(np.asarray(L0_attn_wk, np.float32)),
        "wv": np.ascontiguousarray(np.asarray(L0_attn_wv, np.float32)),
        "wo": np.ascontiguousarray(np.asarray(L0_attn_wo, np.float32)),
        "qwg": np.ascontiguousarray(np.asarray(L0_ffn_qw_gate).astype(np.uint8)),
        "qwu": np.ascontiguousarray(np.asarray(L0_ffn_qw_up).astype(np.uint8)),
        "qwd": np.ascontiguousarray(np.asarray(L0_ffn_qw_down).astype(np.uint8)),
        "zg": np.ascontiguousarray(np.asarray(L0_ffn_z_gate).astype(np.uint8)),
        "zu": np.ascontiguousarray(np.asarray(L0_ffn_z_up).astype(np.uint8)),
        "zd": np.ascontiguousarray(np.asarray(L0_ffn_z_down).astype(np.uint8)),
        "sg": np.ascontiguousarray(np.asarray(L0_ffn_s_gate, np.float32)),
        "su": np.ascontiguousarray(np.asarray(L0_ffn_s_up, np.float32)),
        "sd": np.ascontiguousarray(np.asarray(L0_ffn_s_down, np.float32)),
        "gsel32": _host_gsel(32),
        "gsel86": _host_gsel(86),
        "identf": np.eye(128, dtype=np.float32),
        "identb": np.eye(128, dtype=BF16NP),
    }
    per_h = {}
    for h in (0, 1):
        Ch, Cl, Sh, Sl = _host_rope(h)
        per_h[h] = {"masks": _host_masks(h),
                    "ropeCh": Ch, "ropeCl": Cl, "ropeSh": Sh, "ropeSl": Sl}
    in_maps = []
    for c in range(8):
        b, h = c // 2, c % 2
        if h == 1:
            xkv_a = np.ascontiguousarray(x[b])
        else:
            xkv_a = np.concatenate([np.zeros((1024, D), np.float32), x[b, :1024]], axis=0)
        m = {"xkv": xkv_a}
        m.update(shared)
        m.update(per_h[h])
        in_maps.append(m)
    return in_maps


def kernel(**inputs):
    in_maps = _host_inputs(**inputs)
    nc = build_program(upto=5, dbg=False)
    res = run_bass_kernel_spmd(nc, in_maps, core_ids=list(range(8)))
    outp = np.empty((4, 2048, D), np.float32)
    for c in range(8):
        b, h = c // 2, c % 2
        outp[b, h * 1024:(h + 1) * 1024, :] = res.results[c]["out"]
    return outp
